# version 12
# baseline (speedup 1.0000x reference)
import contextlib
import math

import numpy as np
import ml_dtypes

import concourse.bass as bass
import concourse.mybir as mybir
from concourse.bass_utils import run_bass_kernel_spmd

F32 = mybir.dt.float32
BF16 = mybir.dt.bfloat16
I32 = mybir.dt.int32
AF = mybir.ActivationFunctionType
ALU = mybir.AluOpType

SEQ = 2048
DM = 4096
NKC = 32
NH = 32
HD = 128
OWN = 1024
NQT = 8
NKT = 16
NTOK = 8192
NCL = 64
THR = 0.01
TEMP_INV = 10.0
NEGBIG = -1.0e9


def _alibi_slopes(n_heads: int) -> np.ndarray:
    cp2 = 2 ** math.floor(math.log2(n_heads))
    base = 2 ** (-(2 ** (-(math.log2(cp2) - 3))))
    slopes = base ** np.arange(1, cp2 + 1, dtype=np.float64)
    if cp2 != n_heads:
        extra_base = 2 ** (-(2 ** (-(math.log2(2 * cp2) - 3))))
        n_rem = min(cp2, n_heads - cp2)
        extra = extra_base ** np.arange(1, 1 + 2 * n_rem, 2, dtype=np.float64)
        slopes = np.concatenate([slopes, extra])
    return slopes.astype(np.float32)


SLOPES = [float(s) for s in _alibi_slopes(NH)]
DEDUPE_OFF = False


class Ch:
    def __init__(self, es, nc, name):
        self.h = es.enter_context(nc.semaphore(name))
        self.n = 0

    def inc(self, instr, k=1):
        instr.then_inc(self.h, k)
        self.n += k
        return self.n


class Eng:
    def __init__(self, es, nc, eng, name, auto):
        self.e = eng
        self.ch = Ch(es, nc, name + "c")
        self.auto = auto
        self._waited = {}

    def wait(self, ch, val):
        if val <= 0:
            return
        key = id(ch)
        if not DEDUPE_OFF and self._waited.get(key, -1) >= val:
            return
        assert val <= ch.n, f"wait {val} > produced {ch.n}"
        self.e.wait_ge(ch.h, val)
        self._waited[key] = val

    @staticmethod
    def _maxw(waits):
        d = {}
        for (c, v) in waits:
            k = id(c)
            if k not in d or d[k][1] < v:
                d[k] = (c, v)
        return d.values()

    def op(self, method, *args, waits=(), **kw):
        for (c, v) in self._maxw(waits):
            self.wait(c, v)
        if self.auto and self.ch.n > 0:
            self.wait(self.ch, self.ch.n)
        instr = getattr(self.e, method)(*args, **kw)
        self.ch.inc(instr, 1)
        return self.ch.n

    def raw(self, method, *args, waits=(), **kw):
        for (c, v) in self._maxw(waits):
            self.wait(c, v)
        return getattr(self.e, method)(*args, **kw)

    def mark(self, instr):
        return self.ch.inc(instr, 1)

    def cast(self, out, in_, waits=()):
        m = "copy" if isinstance(self.e, bass.BassScalarEngine) else "tensor_copy"
        return self.op(m, out, in_, waits=waits)

    def dma(self, ch, out, in_, waits=()):
        for (c, v) in self._maxw(waits):
            self.wait(c, v)
        instr = self.e.dma_start(out=out, in_=in_)
        return ch.inc(instr, 16)


def ceil_div(a, b):
    return (a + b - 1) // b


def build_program():
    nc = bass.Bass("TRN2", target_bir_lowering=False, debug=False)

    x_full = nc.dram_tensor("x_full", [SEQ, DM], F32, kind="ExternalInput").ap()
    x_own = nc.dram_tensor("x_own", [OWN, DM], F32, kind="ExternalInput").ap()
    xT_all = nc.dram_tensor("xT_all", [DM, NTOK], F32, kind="ExternalInput").ap()
    w_attn = nc.dram_tensor("w_attn", [DM, 3 * DM], F32, kind="ExternalInput").ap()
    b_attn_t = nc.dram_tensor("b_attn_t", [128, 96], F32, kind="ExternalInput").ap()
    w_proj = nc.dram_tensor("w_proj", [DM, DM], F32, kind="ExternalInput").ap()
    b_proj_t = nc.dram_tensor("b_proj_t", [128, 32], F32, kind="ExternalInput").ap()
    ffn_wT = nc.dram_tensor("ffn_wT", [DM, DM], F32, kind="ExternalInput").ap()
    ffn_b_t = nc.dram_tensor("ffn_b_t", [128, 32], F32, kind="ExternalInput").ap()
    assign_row = nc.dram_tensor("assign_row", [1, DM], I32, kind="ExternalInput").ap()
    centT = nc.dram_tensor("centT", [DM, NCL], F32, kind="ExternalInput").ap()
    bt_in = nc.dram_tensor("bt_in", [NKT, 128, OWN], F32, kind="ExternalInput").ap()
    bt2e_in = nc.dram_tensor("bt2e_in", [NKT, 128, 128], F32,
                             kind="ExternalInput").ap()
    basecol_in = nc.dram_tensor("basecol_in", [128, 32], F32,
                                kind="ExternalInput").ap()
    ident_in = nc.dram_tensor("ident_in", [128, 128], BF16, kind="ExternalInput").ap()

    outT = nc.dram_tensor("outT", [DM, OWN], F32, kind="ExternalOutput").ap()

    xn_own_st = nc.dram_tensor("xn_own_st", [NKC, 128, OWN], BF16).ap()
    xn_fm_st = nc.dram_tensor("xn_fm_st", [NKC, 128, SEQ], BF16).ap()
    k_st = nc.dram_tensor("k_st", [NH, 128, SEQ], BF16).ap()
    v_st = nc.dram_tensor("v_st", [NH, 128, SEQ], BF16).ap()
    q_st = nc.dram_tensor("q_st", [NH, 128, OWN], BF16).ap()
    ofm_st = nc.dram_tensor("ofm_st", [NKC, 128, OWN], BF16).ap()

    es = contextlib.ExitStack()
    with es:
        def sb(name, shape, dt):
            return es.enter_context(nc.sbuf_tensor(name, shape, dt))

        identbf = sb("identbf", [128, 128], BF16)
        identf = sb("identf", [128, 128], F32)
        onescol = sb("onescol", [128, 1], BF16)
        onesrow = sb("onesrow", [1, 128], BF16)
        ones64r = sb("ones64r", [1, 64], BF16)
        battn = sb("battn", [128, 96], F32)
        bq128 = sb("bq128", [128, 32], F32)
        bproj = sb("bproj", [128, 32], F32)
        ffnb = sb("ffnb", [128, 32], F32)
        maskedb = sb("maskedb", [128, 32], F32)
        rs_sb = sb("rs_sb", [128, 32], F32)
        qsel_sb = sb("qsel_sb", [128, 8], F32)
        qselb = sb("qselb", [128, OWN], F32)
        centbf = sb("centbf", [128, NKC * 65], BF16)
        negSbf = sb("negSbf", [1, 64], BF16)
        iota64 = sb("iota64", [64, 1], F32)
        selgt = sb("selgt", [64, 1], F32)
        selbf = sb("selbf", [64, 1], BF16)
        ormax = sb("ormax", [128, 64], F32)
        statc = sb("statc", [128, 8], F32)
        epsrow = sb("epsrow", [1, 1], F32)
        c_cent = sb("c_cent", [128, NCL], F32)

        PE = Eng(es, nc, nc.tensor, "pe", auto=False)
        DV = Eng(es, nc, nc.vector, "dv", auto=True)
        AC = Eng(es, nc, nc.scalar, "ac", auto=True)
        PO = Eng(es, nc, nc.gpsimd, "po", auto=True)
        SY = Eng(es, nc, nc.sync, "sy", auto=False)

        sq = [Ch(es, nc, f"sq{i}") for i in range(10)]
        gq = [Ch(es, nc, f"gq{i}") for i in range(6)]
        wsq = [Ch(es, nc, f"wsq{i}") for i in range(8)]
        wsq2 = [Ch(es, nc, f"wsq2{i}") for i in range(8)]
        dsq = [Ch(es, nc, f"dsq{i}") for i in range(4)]
        posq = [Ch(es, nc, f"posq{i}") for i in range(8)]
        acq = [Ch(es, nc, f"acq{i}") for i in range(3)]
        aq2 = [Ch(es, nc, f"aq2{i}") for i in range(6)]
        wsq3 = [Ch(es, nc, f"wsq3{i}") for i in range(6)]

        cv = centbf[:].rearrange("p (k c) -> p k c", k=NKC)

        def barrier():
            for e in (DV, AC, PO):
                e.op("nop")
            marks = [(e.ch, e.ch.n) for e in (PE, DV, AC, PO)]
            qmarks = [(c, c.n) for c in sq + gq if c.n > 0]
            for e in (PE, DV, AC, PO, SY):
                for (c, v) in marks:
                    if c is not e.ch:
                        e.wait(c, v)
                for (c, v) in qmarks:
                    e.wait(c, v)

        def emit_all():
            SY.dma(sq[0], identbf[:], ident_in)
            SY.dma(sq[0], battn[:], b_attn_t)
            SY.dma(sq[0], bproj[:], b_proj_t)
            SY.dma(sq[0], ffnb[:], ffn_b_t)
            DV.op("memset", onescol[:], 1.0)
            DV.op("memset", onesrow[:], 1.0)
            DV.op("memset", ones64r[:], 1.0)
            PO.op("iota", iota64[:], [[1, 1]], channel_multiplier=1,
                  allow_small_or_imprecise_dtypes=True)
            DV.op("memset", cv[:, :, 64:65], 1.0 / DM)
            DV.op("memset", ormax[:], 0.0)
            DV.op("memset", epsrow[:], 1e-5)
            DV.op("tensor_copy", identf[:], identbf[:], waits=[(sq[0], sq[0].n)])
            DV.op("tensor_scalar", bq128[:], battn[:, 0:32], 1.0 / 128.0, None,
                  ALU.mult)
            with contextlib.ExitStack() as e0c:
                centstage = e0c.enter_context(
                    nc.sbuf_tensor("centstage", [128, NKC * 64], F32))
                csv = centstage[:].rearrange("p (k c) -> p k c", k=NKC)
                SY.dma(sq[1], csv,
                       centT.rearrange("(k p) c -> p k c", p=128))
                DV.op("tensor_copy",
                      cv[:, :, 0:64], csv, waits=[(sq[1], sq[1].n)])
            with contextlib.ExitStack() as e0:
                ps_s64 = e0.enter_context(nc.psum_tensor("ps_s64", [1, 64], F32))
                ins = None
                for kc in range(NKC):
                    ins = PE.raw("matmul", ps_s64[:], onescol[:],
                                 cv[:, kc, 0:64],
                                 start=(kc == 0), stop=(kc == NKC - 1),
                                 waits=[(DV.ch, DV.ch.n)] if kc == 0 else ())
                mS = PE.mark(ins)
                DV.op("tensor_scalar", negSbf[:], ps_s64[:], -1.0, None,
                      ALU.mult, waits=[(PE.ch, mS)])
                barrier()

            with contextlib.ExitStack() as e1:
                e1a = contextlib.ExitStack()
                xt = [e1a.enter_context(nc.sbuf_tensor(f"p1x{s}", [128, DM], F32))
                      for s in range(2)]
                dummy = e1a.enter_context(nc.sbuf_tensor("p1dum", [128, DM], BF16))
                xntm = [e1a.enter_context(nc.sbuf_tensor(f"p1n{s}", [128, DM], BF16))
                        for s in range(2)]
                fmst = [e1a.enter_context(nc.sbuf_tensor(f"p1f{s}", [128, 512], BF16))
                        for s in range(2)]
                ps_t = [e1a.enter_context(nc.psum_tensor(f"p1t{s}", [128, 512], BF16))
                        for s in range(2)]
                x_done = [0, 0]
                xd_dv = [0, 0]
                n_done = [0, 0]
                f_done = [0, 0]
                t_done = [0, 0]

                for t in range(24):
                    own = t >= 16
                    s = t % 2
                    src = (x_own[(t - 16) * 128:(t - 15) * 128, :] if own
                           else x_full[t * 128:(t + 1) * 128, :])
                    wts = [(AC.ch, x_done[s]), (DV.ch, xd_dv[s])]
                    SY.dma(sq[1 + s], xt[s][:, 0:1024], src[:, 0:1024],
                           waits=wts)
                    SY.dma(sq[1 + s], xt[s][:, 1024:2048], src[:, 1024:2048])
                    AC.dma(acq[s], xt[s][:, 2048:3072], src[:, 2048:3072],
                           waits=wts)
                    AC.dma(acq[s], xt[s][:, 3072:4096], src[:, 3072:4096])
                    fills = [(sq[1 + s], sq[1 + s].n),
                             (acq[s], acq[s].n)]
                    xd_dv[s] = DV.op("reduce_sum", statc[:, 0:1], xt[s][:],
                                     axis=mybir.AxisListType.X, waits=fills)
                    AC.op("activation", dummy[:], xt[s][:], AF.Square,
                          accum_out=statc[:, 1:2], waits=fills)
                    DV.op("tensor_scalar", statc[:, 2:3], statc[:, 0:1],
                          1.0 / DM, None, ALU.mult)
                    DV.op("scalar_tensor_tensor", statc[:, 3:4], statc[:, 2:3],
                          -1.0, statc[:, 2:3], ALU.mult, ALU.mult)
                    DV.op("tensor_scalar", statc[:, 3:4], statc[:, 3:4],
                          1e-5, None, ALU.add)
                    AC.op("activation", statc[:, 4:5], statc[:, 1:2], AF.Sqrt,
                          bias=statc[:, 3:4], scale=1.0 / DM,
                          waits=[(DV.ch, DV.ch.n)])
                    DV.op("reciprocal", statc[:, 5:6], statc[:, 4:5],
                          waits=[(AC.ch, AC.ch.n)])
                    DV.op("scalar_tensor_tensor", statc[:, 6:7], statc[:, 2:3],
                          -1.0, statc[:, 5:6], ALU.mult, ALU.mult)
                    x_done[s] = AC.op(
                        "activation", xntm[s][:], xt[s][:], AF.Identity,
                        scale=statc[:, 5:6], bias=statc[:, 6:7],
                        waits=[(DV.ch, DV.ch.n), (PE.ch, n_done[s])])
                    for g in range(8):
                        ts = g % 2
                        ins = None
                        for j in range(4):
                            kc = 4 * g + j
                            ins = PE.raw(
                                "transpose", ps_t[ts][:, j * 128:(j + 1) * 128],
                                xntm[s][:, kc * 128:(kc + 1) * 128], identbf[:],
                                waits=([(AC.ch, x_done[s]), (DV.ch, t_done[ts])]
                                       if j == 0 else ()))
                        mpe = PE.mark(ins)
                        t_done[ts] = DV.op(
                            "tensor_copy", fmst[ts][:], ps_t[ts][:],
                            waits=[(PE.ch, mpe), (gq[ts], f_done[ts])])
                        fv = fmst[ts][:].rearrange("p (b c) -> p b c", b=4)
                        if own:
                            tq = t - 16
                            dst = xn_own_st[4 * g:4 * g + 4, :,
                                            tq * 128:(tq + 1) * 128]
                        else:
                            dst = xn_fm_st[4 * g:4 * g + 4, :,
                                           t * 128:(t + 1) * 128]
                        dst = dst.rearrange("b p c -> p b c")
                        PO.dma(gq[ts], dst, fv, waits=[(DV.ch, t_done[ts])])
                        f_done[ts] = gq[ts].n
                    n_done[s] = PE.ch.n
                barrier()
                e1a.close()

            with contextlib.ExitStack() as e2:
                xno = e2.enter_context(
                    nc.sbuf_tensor("xno", [128, NKC * OWN], BF16))
                xnov = xno[:].rearrange("p (k t) -> p k t", k=NKC)
                for kc in range(NKC):
                    SY.dma(sq[6], xnov[:, kc, :], xn_own_st[kc])
                for e in (PE, DV, AC):
                    e.wait(sq[6], sq[6].n)

                with contextlib.ExitStack() as e23:
                    wt = [e23.enter_context(
                        nc.sbuf_tensor(f"qw{s}", [128, 256], F32))
                        for s in range(6)]
                    wb = [e23.enter_context(
                        nc.sbuf_tensor(f"qb{s}", [128, 256], BF16))
                        for s in range(6)]
                    qout = [e23.enter_context(
                        nc.sbuf_tensor(f"qo{s}", [128, OWN], BF16))
                        for s in range(2)]
                    xtt = [e23.enter_context(
                        nc.sbuf_tensor(f"d3x{s}", [128, 512], F32))
                        for s in range(6)]
                    xbb = [e23.enter_context(
                        nc.sbuf_tensor(f"d3b{s}", [128, 512], BF16))
                        for s in range(6)]
                    xsq = [e23.enter_context(
                        nc.sbuf_tensor(f"d3q{s}", [128, 512], BF16))
                        for s in range(6)]
                    rows = e23.enter_context(
                        nc.sbuf_tensor("d3r", [1, 6 * 512], F32))
                    rowbf = e23.enter_context(
                        nc.sbuf_tensor("d3rb", [1, 512], BF16))
                    rawsb = e23.enter_context(
                        nc.sbuf_tensor("d3w", [64, 512], BF16))
                    dtok = e23.enter_context(
                        nc.sbuf_tensor("d3t", [128, 256], BF16))
                    rstdc = e23.enter_context(
                        nc.sbuf_tensor("d3s", [128, 4], F32))
                    rstdb = e23.enter_context(
                        nc.sbuf_tensor("d3sb", [1, 512], BF16))
                    ed = e23.enter_context(nc.sbuf_tensor("d3e", [128, 64], F32))
                    gt = e23.enter_context(nc.sbuf_tensor("d3g", [128, 64], F32))
                    ohbf = e23.enter_context(
                        nc.sbuf_tensor("d3ob", [64, 512], BF16))
                    ps_d2 = [e23.enter_context(
                        nc.psum_tensor("d3pd0", [128, 512], F32))]
                    ps_t3b = e23.enter_context(
                        nc.psum_tensor("d3ptb", [128, 384], BF16))
                    ekv = contextlib.ExitStack()
                    kvh = ekv.enter_context(
                        nc.sbuf_tensor("kvh", [128, NKC * 1024], BF16))
                    kvhv = kvh[:].rearrange("p (k t) -> p k t", k=NKC)
                    wt2 = [ekv.enter_context(
                        nc.sbuf_tensor(f"kw{s}", [128, 128], F32))
                        for s in range(6)]
                    wb2 = [ekv.enter_context(
                        nc.sbuf_tensor(f"kb{s}", [128, 128], BF16))
                        for s in range(6)]
                    kvout = [ekv.enter_context(
                        nc.sbuf_tensor(f"kvo{s}", [128, 1024], BF16))
                        for s in range(2)]
                    ps_kv = [ekv.enter_context(
                        nc.psum_tensor(f"p2p{i}", [128, 512], F32))
                        for i in range(2)]
                    e2bq = contextlib.ExitStack()
                    ps_q = [e2bq.enter_context(
                        nc.psum_tensor(f"qp{i}", [128, 512], F32))
                        for i in range(4)]

                    def p2b_gen():
                        casters = [DV, AC, DV, AC, DV, AC]
                        w_done = [(DV.ch, 0)] * 6
                        b_done = [0] * 6
                        o_done = [0, 0]
                        ev_done = [0, 0]
                        for fp_ in range(NH // 2):
                            ins = None
                            for kc in range(NKC):
                                s = (fp_ * NKC + kc) % 6
                                ce = casters[s]
                                qeng, qch = ((SY, wsq[s]) if s % 2 == 0
                                             else (AC, aq2[s]))
                                qeng.dma(qch, wt[s][:],
                                         w_attn[kc * 128:(kc + 1) * 128,
                                                fp_ * 256:fp_ * 256 + 256],
                                         waits=[w_done[s]])
                                w_done[s] = (ce.ch, ce.cast(
                                    wb[s][:], wt[s][:],
                                    waits=[(qch, qch.n),
                                           (PE.ch, b_done[s])]))
                                for sub in range(2):
                                    for th in range(2):
                                        ins = PE.raw(
                                            "matmul",
                                            ps_q[2 * sub + th][:],
                                            wb[s][:, sub * 128:
                                                 (sub + 1) * 128],
                                            xnov[:, kc,
                                                 th * 512:(th + 1) * 512],
                                            start=(kc == 0),
                                            stop=(kc == NKC - 1),
                                            waits=([w_done[s],
                                                    (AC.ch, ev_done[0])]
                                                   if sub == 0 and th == 0
                                                   else ()))
                                b_done[s] = PE.mark(ins)
                            pe_g = PE.ch.n
                            for sub in range(2):
                                f = 2 * fp_ + sub
                                for th in range(2):
                                    ev_done[0] = AC.op(
                                        "activation",
                                        qout[sub][:, th * 512:(th + 1) * 512],
                                        ps_q[2 * sub + th][:],
                                        AF.Identity,
                                        bias=bq128[:, f:f + 1],
                                        scale=1.0 / 128.0,
                                        waits=[(PE.ch, pe_g),
                                               (gq[2 + sub], o_done[sub])])
                                PO.dma(gq[2 + sub], q_st[f], qout[sub][:],
                                       waits=[(AC.ch, ev_done[0]),
                                              (gq[2 + sub], gq[2 + sub].n)])
                                o_done[sub] = gq[2 + sub].n
                            yield

                    def p3_gen():
                        x_done = [0] * 6
                        b_done3 = [0] * 6
                        q_done3 = [(AC.ch, 0), (PO.ch, 0), (AC.ch, 0),
                                   (PO.ch, 0), (AC.ch, 0), (PO.ch, 0)]
                        psd_free_ac = [0, 0]
                        psd_free_dv = [0, 0]
                        raws_free = [0]
                        for tg in range(16):
                            ps_d = ps_d2[0]
                            ins = None
                            for kc in range(NKC):
                                s = (tg * NKC + kc) % 6
                                sqe = AC if s % 2 == 0 else PO
                                xqe, xch = ((SY, dsq[s]) if s < 3
                                            else (AC, acq[s - 3]))
                                xqe.dma(xch, xtt[s][:],
                                        xT_all[kc * 128:(kc + 1) * 128,
                                               tg * 512:(tg + 1) * 512],
                                        waits=[(DV.ch, b_done3[s]),
                                               q_done3[s]])
                                b_done3[s] = DV.op(
                                    "tensor_copy", xbb[s][:], xtt[s][:],
                                    waits=[(xch, xch.n),
                                           (PE.ch, x_done[s])])
                                if sqe is AC:
                                    q_done3[s] = (AC.ch, AC.op(
                                        "activation", xsq[s][:], xtt[s][:],
                                        AF.Square,
                                        waits=[(xch, xch.n),
                                               (PE.ch, x_done[s])]))
                                else:
                                    q_done3[s] = (PO.ch, PO.op(
                                        "tensor_tensor", xsq[s][:], xtt[s][:],
                                        xtt[s][:], ALU.mult,
                                        waits=[(xch, xch.n),
                                               (PE.ch, x_done[s])]))
                                PE.raw("matmul", ps_d[0:64, :],
                                       cv[:, kc, 0:64], xbb[s][:],
                                       start=(kc == 0), stop=(kc == NKC - 1),
                                       waits=([(DV.ch, b_done3[s]),
                                               q_done3[s],
                                               (AC.ch, psd_free_ac[0]),
                                               (DV.ch, psd_free_dv[0])]
                                              if kc == 0 else
                                              [(DV.ch, b_done3[s]),
                                               q_done3[s]]))
                                PE.raw("matmul", ps_d[64:65, :], onescol[:],
                                       xbb[s][:],
                                       start=(kc == 0), stop=(kc == NKC - 1),
                                       tile_position=(0, 64),
                                       skip_group_check=True)
                                ins = PE.raw("matmul", ps_d[96:97, :],
                                             onescol[:], xsq[s][:],
                                             start=(kc == 0),
                                             stop=(kc == NKC - 1),
                                             tile_position=(0, 96),
                                             skip_group_check=True)
                                x_done[s] = PE.mark(ins)
                            mmm = PE.ch.n
                            DV.op("tensor_scalar", rows[:, 0:512],
                                  ps_d[64:65, :], 1.0 / DM, None, ALU.mult,
                                  waits=[(PE.ch, mmm)])
                            DV.op("tensor_copy", rows[:, 512:1024],
                                  ps_d[96:97, :])
                            DV.op("tensor_copy", rowbf[:], rows[:, 0:512])
                            DV.op("tensor_tensor", rows[:, 1024:1536],
                                  rows[:, 0:512], rows[:, 0:512], ALU.mult)
                            DV.op("scalar_tensor_tensor", rows[:, 1536:2048],
                                  rows[:, 512:1024], 1.0 / DM,
                                  rows[:, 1024:1536],
                                  ALU.mult, ALU.subtract)
                            AC.op("activation", rows[:, 2048:2560],
                                  rows[:, 1536:2048], AF.Sqrt, bias=epsrow[:],
                                  scale=1.0,
                                  waits=[(DV.ch, DV.ch.n)])
                            DV.op("reciprocal", rows[:, 2560:3072],
                                  rows[:, 2048:2560], waits=[(AC.ch, AC.ch.n)])
                            DV.op("tensor_scalar", rows[:, 2560:3072],
                                  rows[:, 2560:3072], TEMP_INV, None, ALU.mult)
                            DV.op("tensor_copy", rstdb[:], rows[:, 2560:3072])
                            ins = PE.raw("matmul", ps_d[0:64, :], negSbf[:],
                                         rowbf[:],
                                         start=False, stop=True,
                                         skip_group_check=True,
                                         waits=[(DV.ch, DV.ch.n)])
                            m2 = PE.mark(ins)
                            for j in range(4):
                                ins = PE.raw(
                                    "transpose",
                                    ps_t3b[:, 256 + 2 * j:257 + 2 * j],
                                    rstdb[:, j * 128:(j + 1) * 128],
                                    identbf[0:1, 0:1],
                                    waits=[(DV.ch, DV.ch.n)] if j == 0 else ())
                            m3 = PE.mark(ins)
                            rsv = ps_t3b[:, 256:264].rearrange(
                                "p (a b) -> p a b", b=2)
                            DV.op("tensor_copy", rstdc[:], rsv[:, :, 0],
                                  waits=[(PE.ch, m3)])
                            psd_free_ac[0] = AC.op(
                                "activation", rawsb[:], ps_d[0:64, :],
                                AF.Identity,
                                bias=0.0, scale=1.0,
                                waits=[(PE.ch, m2), (PE.ch, raws_free[0])])
                            for j in range(4):
                                ins = PE.raw(
                                    "transpose", ps_t3b[:, j * 64:(j + 1) * 64],
                                    rawsb[:, j * 128:(j + 1) * 128],
                                    identbf[0:64, 0:64],
                                    waits=([(AC.ch, psd_free_ac[0]),
                                            (DV.ch, DV.ch.n)]
                                           if j == 0 else ()))
                            m4 = raws_free[0] = PE.mark(ins)
                            psd_free_dv[0] = DV.op(
                                "tensor_copy", dtok[:], ps_t3b[:, 0:256],
                                waits=[(PE.ch, m4)])
                            for j in range(4):
                                AC.op("activation", ed[:],
                                      dtok[:, j * 64:(j + 1) * 64],
                                      AF.Exp, scale=rstdc[:, j:j + 1],
                                      accum_out=statc[:, 7:8],
                                      waits=[(DV.ch, DV.ch.n)])
                                DV.op("tensor_scalar", statc[:, 7:8],
                                      statc[:, 7:8],
                                      THR, None, ALU.mult,
                                      waits=[(AC.ch, AC.ch.n)])
                                DV.op("tensor_scalar", gt[:], ed[:],
                                      statc[:, 7:8], None, ALU.is_gt)
                                DV.op("tensor_tensor", ormax[:], ormax[:],
                                      gt[:], ALU.max)
                                if tg < 2:
                                    DV.op("reduce_max",
                                          qsel_sb[:, tg * 4 + j:
                                                  tg * 4 + j + 1],
                                          gt[:], axis=mybir.AxisListType.X)
                            yield

                    def p2a_gen(half):
                        for kc in range(NKC):
                            SY.dma(wsq3[kc % 3], kvhv[:, kc, :],
                                   xn_fm_st[kc, :,
                                            half * 1024:(half + 1) * 1024])
                        kvh_m = [(wsq3[i], wsq3[i].n) for i in range(3)]
                        casters = [DV, PO, DV, PO, DV, PO]
                        w_done = [(DV.ch, 0)] * 6
                        b_done = [0] * 6
                        o_done = [0, 0]
                        ev_ac = 0
                        ev_dv = 0
                        forder = []
                        for hh in range(0, NH, 2):
                            forder += [hh, hh + 1, NH + hh, NH + hh + 1]
                        for fi, f in enumerate(forder):
                            col0 = DM + f * 128
                            ins = None
                            for kc in range(NKC):
                                s = (fi * NKC + kc) % 6
                                ce = casters[s]
                                qeng, qch = ((SY, wsq3[3 + s // 2])
                                             if s % 2 == 0
                                             else (AC, aq2[s]))
                                qeng.dma(qch, wt2[s][:],
                                         w_attn[kc * 128:(kc + 1) * 128,
                                                col0:col0 + 128],
                                         waits=[w_done[s]])
                                w_done[s] = (ce.ch, ce.cast(
                                    wb2[s][:], wt2[s][:],
                                    waits=[(qch, qch.n),
                                           (PE.ch, b_done[s])]))
                                for th in range(2):
                                    ins = PE.raw(
                                        "matmul", ps_kv[th][:], wb2[s][:],
                                        kvhv[:, kc, th * 512:(th + 1) * 512],
                                        start=(kc == 0), stop=(kc == NKC - 1),
                                        waits=([w_done[s], (AC.ch, ev_ac),
                                                (DV.ch, ev_dv)] + kvh_m
                                               if th == 0 else ()))
                                b_done[s] = PE.mark(ins)
                            pe_g = PE.ch.n
                            osl = fi % 2
                            bias = battn[:, 32 + f:33 + f]
                            ev_ac = AC.op(
                                "activation", kvout[osl][:, 0:512],
                                ps_kv[0][:], AF.Identity, bias=bias, scale=1.0,
                                waits=[(PE.ch, pe_g), (gq[0], o_done[osl])])
                            ev_dv = DV.op(
                                "tensor_scalar", kvout[osl][:, 512:1024],
                                ps_kv[1][:], 1.0, bias, ALU.mult, ALU.add,
                                waits=[(PE.ch, pe_g), (gq[0], o_done[osl])])
                            dst = (k_st[f] if f < NH else v_st[f - NH])
                            PO.dma(gq[0],
                                   dst[:, half * 1024:(half + 1) * 1024],
                                   kvout[osl][:],
                                   waits=[(AC.ch, ev_ac), (DV.ch, ev_dv),
                                          (gq[0], gq[0].n)])
                            o_done[osl] = gq[0].n
                            yield

                    g2 = p2b_gen()
                    g3 = p3_gen()
                    ga = p2a_gen(0)
                    done = object()
                    for i in range(16):
                        next(g2, done)
                        next(g3, done)
                        next(ga, done)
                        next(ga, done)
                        next(ga, done)
                        next(ga, done)
                    while next(g2, done) is not done:
                        pass
                    while next(g3, done) is not done:
                        pass
                    while next(ga, done) is not done:
                        pass
                    barrier()
                    gb = p2a_gen(1)
                    while next(gb, done) is not done:
                        pass
                    barrier()
                    e2bq.close()
                    ekv.close()
                    ps_t3 = e23.enter_context(
                        nc.psum_tensor("d3pt", [128, 512], F32))
                    ps_oh = e23.enter_context(
                        nc.psum_tensor("d3oh", [64, 512], F32))
                    ps_rs = e23.enter_context(
                        nc.psum_tensor("d3rs", [1, 512], F32))
                    asgrow = e23.enter_context(
                        nc.sbuf_tensor("asgrow", [1, DM], BF16))
                    asg_i = e23.enter_context(
                        nc.sbuf_tensor("asg_i", [1, DM], I32))
                    rsrow = e23.enter_context(
                        nc.sbuf_tensor("rsrow", [1, DM], F32))
                    qrow = e23.enter_context(
                        nc.sbuf_tensor("qrow", [1, OWN], BF16))
                    SY.dma(sq[1], asg_i[:], assign_row)
                    DV.op("tensor_copy", asgrow[:], asg_i[:],
                          waits=[(sq[1], sq[1].n)])
                    ins = PE.raw("transpose", ps_t3[0:64, 0:128], ormax[:],
                                 identf[:], waits=[(DV.ch, DV.ch.n)])
                    mT = PE.mark(ins)
                    DV.op("reduce_max", selgt[:], ps_t3[0:64, 0:128],
                          axis=mybir.AxisListType.X, waits=[(PE.ch, mT)])
                    DV.op("tensor_scalar", selgt[:], selgt[:], 0.5, None,
                          ALU.is_gt)
                    DV.op("tensor_copy", selbf[:], selgt[:])
                    for c in range(8):
                        ins = PE.raw("matmul", ps_oh[:], ones64r[:],
                                     asgrow[:, c * 512:(c + 1) * 512],
                                     start=True, stop=True,
                                     waits=[(DV.ch, DV.ch.n)])
                        m5 = PE.mark(ins)
                        DV.op("tensor_scalar", ohbf[:], ps_oh[:], iota64[:],
                              None, ALU.is_equal, waits=[(PE.ch, m5)])
                        ins = PE.raw("matmul", ps_rs[:], selbf[:], ohbf[:],
                                     start=True, stop=True,
                                     waits=[(DV.ch, DV.ch.n)])
                        m6 = PE.mark(ins)
                        DV.op("tensor_copy", rsrow[:, c * 512:(c + 1) * 512],
                              ps_rs[:], waits=[(PE.ch, m6)])
                    for j in range(32):
                        ins = PE.raw(
                            "transpose", ps_t3[:, j:j + 1],
                            rsrow[:, j * 128:(j + 1) * 128], identf[0:1, 0:1],
                            waits=[(DV.ch, DV.ch.n)] if j == 0 else ())
                    m7 = PE.mark(ins)
                    DV.op("tensor_copy", rs_sb[:], ps_t3[:, 0:32],
                          waits=[(PE.ch, m7)])
                    DV.op("tensor_tensor", maskedb[:], ffnb[:], rs_sb[:],
                          ALU.mult)
                    for g in range(2):
                        ins = None
                        for j in range(4):
                            jj = g * 4 + j
                            ins = PE.raw(
                                "transpose", ps_t3[0:1, j * 128:(j + 1) * 128],
                                qsel_sb[:, jj:jj + 1], identf[:],
                                waits=[(DV.ch, DV.ch.n)] if j == 0 else ())
                        m8 = PE.mark(ins)
                        DV.op("tensor_copy", qrow[:, g * 512:(g + 1) * 512],
                              ps_t3[0:1, 0:512], waits=[(PE.ch, m8)])
                    for g in range(2):
                        ins = PE.raw("matmul", ps_t3[:],
                                     onesrow[:],
                                     qrow[:, g * 512:(g + 1) * 512],
                                     start=True, stop=True,
                                     waits=[(DV.ch, DV.ch.n)])
                        m9 = PE.mark(ins)
                        DV.op("tensor_copy", qselb[:, g * 512:(g + 1) * 512],
                              ps_t3[:], waits=[(PE.ch, m9)])
                    barrier()

                with contextlib.ExitStack() as e4:
                    btsb = e4.enter_context(
                        nc.sbuf_tensor("btsb", [128, NKT * OWN], F32))
                    btv = btsb[:].rearrange("p (k t) -> p k t", k=NKT)
                    bt2e = e4.enter_context(
                        nc.sbuf_tensor("bt2e", [128, NKT * 128], F32))
                    bt2v = bt2e[:].rearrange("p (k t) -> p k t", k=NKT)
                    basecol = e4.enter_context(
                        nc.sbuf_tensor("basecol", [128, 32], F32))
                    biasall = e4.enter_context(
                        nc.sbuf_tensor("biasall", [128, 32], F32))
                    qh = [e4.enter_context(
                        nc.sbuf_tensor(f"a4q{s}", [128, OWN], BF16))
                        for s in range(2)]
                    kh = [e4.enter_context(
                        nc.sbuf_tensor(f"a4k{s}", [128, SEQ], BF16))
                        for s in range(2)]
                    vh = [e4.enter_context(
                        nc.sbuf_tensor(f"a4v{s}", [128, SEQ], BF16))
                        for s in range(2)]
                    vaug = [e4.enter_context(
                        nc.sbuf_tensor(f"a4a{s}", [128, NKT * 129], BF16))
                        for s in range(2)]
                    ebuf = [e4.enter_context(
                        nc.sbuf_tensor(f"a4e{s}", [128, 512], BF16))
                        for s in range(4)]
                    tbuf = [e4.enter_context(
                        nc.sbuf_tensor(f"a4t{s}", [128, 512], F32))
                        for s in range(3)]
                    obuf = e4.enter_context(nc.sbuf_tensor("a4o", [128, OWN], BF16))
                    otsb = [e4.enter_context(
                        nc.sbuf_tensor(f"a4s{s}", [128, 512], BF16))
                        for s in range(2)]
                    recipb = e4.enter_context(nc.sbuf_tensor("a4r", [128, 8], F32))
                    ps_s = [e4.enter_context(
                        nc.psum_tensor(f"a4ps{i}", [128, 512], F32))
                        for i in range(3)]
                    ps_o4 = [e4.enter_context(
                        nc.psum_tensor(f"a4po{j}", [128, 512], F32))
                        for j in range(4)]
                    ps_tr0 = e4.enter_context(
                        nc.psum_tensor("a4pt", [128, 512], BF16))
                    for kt in range(NKT):
                        SY.dma(sq[6], btv[:, kt, :], bt_in[kt])
                        SY.dma(sq[6], bt2v[:, kt, :], bt2e_in[kt])
                    SY.dma(sq[6], basecol[:], basecol_in)
                    bt_m = sq[6].n
                    for s in range(2):
                        av = vaug[s][:].rearrange("p (k c) -> p k c", k=NKT)
                        DV.op("memset", av[:, :, 128:129], 1.0)
                    q_done = [0, 0]
                    k_done = [0, 0]
                    v_done = [0, 0]
                    a_done = [0, 0]
                    e_done = [0, 0, 0, 0]
                    t_done = [0, 0, 0]
                    s_done_dv = [0, 0, 0]
                    s_done_ac = [0, 0, 0]
                    bias_ready = 0
                    bias_free = 0
                    tr_done = [0, 0]
                    ot_done = [0, 0]
                    ps_o_free = 0
                    obuf_free = 0
                    pcnt = 0
                    for h in range(NH):
                        s = h % 2
                        SY.dma(sq[4 + s], qh[s][:], q_st[h],
                               waits=[(PE.ch, q_done[s])])
                        SY.dma(sq[4 + s], kh[s][:], k_st[h],
                               waits=[(PE.ch, k_done[s])])
                        SY.dma(sq[4 + s], vh[s][:], v_st[h],
                               waits=[(PE.ch, v_done[s])])
                        fillm = sq[4 + s].n
                        av = vaug[s][:].rearrange("p (k c) -> p k c", k=NKT)
                        for g in range(4):
                            ins = None
                            for j in range(4):
                                kt = 4 * g + j
                                ins = PE.raw(
                                    "transpose",
                                    ps_tr0[:, j * 128:(j + 1) * 128],
                                    vh[s][:, kt * 128:(kt + 1) * 128],
                                    identbf[:],
                                    waits=([(sq[4 + s], fillm),
                                            (DV.ch, tr_done[0]),
                                            (PE.ch, a_done[s])]
                                           if j == 0 else ()))
                            mpe = PE.mark(ins)
                            pv = ps_tr0[:].rearrange("p (b c) -> p b c", b=4)
                            tr_done[0] = DV.op(
                                "tensor_copy", av[:, 4 * g:4 * g + 4, 0:128], pv,
                                waits=[(PE.ch, mpe)])
                        v_done[s] = PE.ch.n
                        vaug_ready = DV.ch.n
                        steep = SLOPES[h] * 448.0 > 40.0
                        if not steep:
                            DV.op("tensor_scalar", biasall[:], basecol[:],
                                  SLOPES[h], None, ALU.mult,
                                  waits=[(sq[6], bt_m), (AC.ch, bias_free)])
                            bias_ready = DV.ch.n
                        for half in range(2):
                            ilo = 4 * half
                            ihi = ilo + 4
                            pend_av = []

                            def do_av(kt, exp_m, imin_h):
                                nonlocal pcnt
                                ins = None
                                for i in range(imin_h, ihi):
                                    ins = PE.raw(
                                        "matmul", ps_o4[i - ilo][:, 0:129],
                                        ebuf[kt % 4][:, (i - ilo) * 128:
                                                     (i - ilo + 1) * 128],
                                        av[:, kt, :],
                                        start=(kt == 0), stop=(kt == 2 * i + 1),
                                        skip_group_check=True,
                                        waits=(exp_m +
                                               [(DV.ch, vaug_ready),
                                                (AC.ch, ps_o_free)]
                                               if i == imin_h else ()))
                                return PE.mark(ins)

                            for kt in range(2 * ihi):
                                imin_h = max(kt // 2, ilo)
                                c0 = imin_h * 128
                                cw = ihi * 128 - c0
                                eb = kt % 4
                                p = pcnt % 3
                                pcnt += 1
                                ins = PE.raw(
                                    "matmul", ps_s[p][:, 0:cw],
                                    kh[s][:, kt * 128:(kt + 1) * 128],
                                    qh[s][:, c0:c0 + cw],
                                    start=True, stop=True,
                                    waits=[(sq[4 + s], fillm),
                                           (DV.ch, s_done_dv[p]),
                                           (AC.ch, s_done_ac[p])])
                                mS = PE.mark(ins)
                                el0 = c0 - ilo * 128
                                has_edge = kt >= 2 * ilo
                                exp_waits = []
                                if steep or not has_edge and steep:
                                    pass
                                if steep:
                                    s_done_dv[p] = DV.op(
                                        "scalar_tensor_tensor", tbuf[p][:, 0:cw],
                                        btv[:, kt, c0:c0 + cw], SLOPES[h],
                                        ps_s[p][:, 0:cw], ALU.mult, ALU.add,
                                        waits=[(PE.ch, mS), (sq[6], bt_m),
                                               (AC.ch, t_done[p])])
                                    t_done[p] = AC.op(
                                        "activation", ebuf[eb][:, el0:el0 + cw],
                                        tbuf[p][:, 0:cw], AF.Exp,
                                        waits=[(DV.ch, s_done_dv[p]),
                                               (PE.ch, e_done[eb])])
                                    s_done_ac[p] = t_done[p]
                                    exp_waits = [(AC.ch, t_done[p])]
                                else:
                                    bcol = biasall[:, half * 16 + kt:
                                                   half * 16 + kt + 1]
                                    if has_edge:
                                        s_done_dv[p] = DV.op(
                                            "scalar_tensor_tensor",
                                            tbuf[p][:, 0:128],
                                            bt2v[:, kt, :], SLOPES[h],
                                            ps_s[p][:, 0:128],
                                            ALU.mult, ALU.add,
                                            waits=[(PE.ch, mS), (sq[6], bt_m),
                                                   (AC.ch, t_done[p])])
                                        t_done[p] = AC.op(
                                            "activation",
                                            ebuf[eb][:, el0:el0 + 128],
                                            tbuf[p][:, 0:128], AF.Exp,
                                            waits=[(DV.ch, s_done_dv[p]),
                                                   (PE.ch, e_done[eb])])
                                        if cw > 128:
                                            t_done[p] = AC.op(
                                                "activation",
                                                ebuf[eb][:, el0 + 128:el0 + cw],
                                                ps_s[p][:, 128:cw], AF.Exp,
                                                bias=bcol, scale=1.0,
                                                waits=[(DV.ch, bias_ready)])
                                        s_done_ac[p] = t_done[p]
                                        exp_waits = [(AC.ch, t_done[p])]
                                    else:
                                        t_done[p] = AC.op(
                                            "activation",
                                            ebuf[eb][:, el0:el0 + cw],
                                            ps_s[p][:, 0:cw], AF.Exp,
                                            bias=bcol, scale=1.0,
                                            waits=[(PE.ch, mS),
                                                   (DV.ch, bias_ready),
                                                   (PE.ch, e_done[eb])])
                                        s_done_ac[p] = t_done[p]
                                        exp_waits = [(AC.ch, t_done[p])]
                                pend_av.append((kt, imin_h, exp_waits))
                                if len(pend_av) > 2:
                                    pkt, pimin, pw = pend_av.pop(0)
                                    e_done[pkt % 4] = a_done[s] = do_av(
                                        pkt, pw, pimin)
                            while pend_av:
                                pkt, pimin, pw = pend_av.pop(0)
                                e_done[pkt % 4] = a_done[s] = do_av(
                                    pkt, pw, pimin)
                            last_av = PE.ch.n
                            for j in range(4):
                                DV.op("reciprocal", recipb[:, ilo + j:ilo + j + 1],
                                      ps_o4[j][:, 128:129],
                                      waits=[(PE.ch, last_av),
                                             (AC.ch, ps_o_free)])
                            for j in range(4):
                                i = ilo + j
                                DV.op("tensor_scalar",
                                      obuf[:, i * 128:(i + 1) * 128],
                                      ps_o4[j][:, 0:128],
                                      recipb[:, i:i + 1], None, ALU.mult,
                                      waits=[(PE.ch, obuf_free)])
                            AC.op("nop", waits=[(DV.ch, DV.ch.n)])
                            ps_o_free = AC.ch.n
                        bias_free = AC.ch.n
                        for g in range(2):
                            ts = g % 2
                            ins = None
                            for j in range(4):
                                i = 4 * g + j
                                ins = PE.raw(
                                    "transpose",
                                    ps_tr0[:, j * 128:(j + 1) * 128],
                                    obuf[:, i * 128:(i + 1) * 128], identbf[:],
                                    waits=([(AC.ch, ps_o_free),
                                            (DV.ch, tr_done[0])]
                                           if j == 0 else ()))
                            mpe = PE.mark(ins)
                            tr_done[0] = DV.op(
                                "tensor_copy", otsb[ts][:], ps_tr0[:],
                                waits=[(PE.ch, mpe), (gq[2 + ts], ot_done[ts])])
                            PO.dma(gq[2 + ts],
                                   ofm_st[h, :, g * 512:(g + 1) * 512],
                                   otsb[ts][:], waits=[(DV.ch, tr_done[0])])
                            ot_done[ts] = gq[2 + ts].n
                        obuf_free = PE.ch.n
                        q_done[s] = k_done[s] = PE.ch.n
                    barrier()

                with contextlib.ExitStack() as e5:
                    ofm = e5.enter_context(
                        nc.sbuf_tensor("ofm", [128, NKC * OWN], BF16))
                    ofv = ofm[:].rearrange("p (k t) -> p k t", k=NKC)
                    for kc in range(NKC):
                        SY.dma(sq[6], ofv[:, kc, :], ofm_st[kc])
                    ofm_m = sq[6].n
                    NSL = 6
                    wpt = [e5.enter_context(
                        nc.sbuf_tensor(f"f5p{s}", [128, 256], F32))
                        for s in range(NSL)]
                    wpb = [e5.enter_context(
                        nc.sbuf_tensor(f"f5pb{s}", [128, 256], BF16))
                        for s in range(NSL)]
                    wft = [e5.enter_context(
                        nc.sbuf_tensor(f"f5f{s}", [128, 256], F32))
                        for s in range(NSL)]
                    wfb = [e5.enter_context(
                        nc.sbuf_tensor(f"f5fb{s}", [128, 256], BF16))
                        for s in range(NSL)]
                    xres = [e5.enter_context(
                        nc.sbuf_tensor(f"f5x{s}", [128, 512], F32))
                        for s in range(2)]
                    t1 = [e5.enter_context(
                        nc.sbuf_tensor(f"f5t1{i}", [128, 512], F32))
                        for i in range(2)]
                    t2 = [e5.enter_context(
                        nc.sbuf_tensor(f"f5t2{i}", [128, 512], F32))
                        for i in range(2)]
                    t3 = e5.enter_context(nc.sbuf_tensor("f5t3", [128, 512], F32))
                    osb = [e5.enter_context(
                        nc.sbuf_tensor(f"f5o{s}", [128, 512], F32))
                        for s in range(2)]
                    ps_a = [e5.enter_context(
                        nc.psum_tensor(f"f5a{i}", [128, 512], F32))
                        for i in range(4)]
                    ps_b = [e5.enter_context(
                        nc.psum_tensor(f"f5b{i}", [128, 512], F32))
                        for i in range(4)]
                    casters = [DV, AC, DV, AC, DV, AC]
                    wp_done = [(DV.ch, 0)] * NSL
                    wf_done = [(DV.ch, 0)] * NSL
                    pb_done = [0] * NSL
                    fb_done = [0] * NSL
                    x_done = [0, 0]
                    o_done = [0, 0]
                    t_free = [0, 0]
                    ev_ac = 0
                    ev_dv = 0
                    for fp_ in range(NKC // 2):
                        for kc in range(NKC):
                            s = (fp_ * NKC + kc) % NSL
                            ce = casters[s]
                            SY.dma(wsq[s], wpt[s][:],
                                   w_proj[kc * 128:(kc + 1) * 128,
                                          fp_ * 256:(fp_ + 1) * 256],
                                   waits=[wp_done[s]])
                            wp_done[s] = (ce.ch, ce.cast(
                                wpb[s][:], wpt[s][:],
                                waits=[(wsq[s], wsq[s].n),
                                       (PE.ch, pb_done[s])]))
                            AC.dma(aq2[s], wft[s][:],
                                   ffn_wT[kc * 128:(kc + 1) * 128,
                                          fp_ * 256:(fp_ + 1) * 256],
                                   waits=[wf_done[s]])
                            wf_done[s] = (ce.ch, ce.cast(
                                wfb[s][:], wft[s][:],
                                waits=[(aq2[s], aq2[s].n),
                                       (PE.ch, fb_done[s])]))
                            ins = None
                            for sub in range(2):
                                for rc in range(2):
                                    ins = PE.raw(
                                        "matmul", ps_a[2 * sub + rc][:],
                                        wpb[s][:, sub * 128:(sub + 1) * 128],
                                        ofv[:, kc, rc * 512:(rc + 1) * 512],
                                        start=(kc == 0),
                                        stop=(kc == NKC - 1),
                                        waits=([wp_done[s],
                                                (AC.ch, ev_ac),
                                                (sq[6], ofm_m)]
                                               if sub == 0 and rc == 0
                                               else ()))
                            pb_done[s] = PE.mark(ins)
                            for sub in range(2):
                                for rc in range(2):
                                    ins = PE.raw(
                                        "matmul", ps_b[2 * sub + rc][:],
                                        wfb[s][:, sub * 128:(sub + 1) * 128],
                                        xnov[:, kc, rc * 512:(rc + 1) * 512],
                                        start=(kc == 0),
                                        stop=(kc == NKC - 1),
                                        waits=([wf_done[s]]
                                               if sub == 0 and rc == 0
                                               else ()))
                            fb_done[s] = PE.mark(ins)
                        mm_done = PE.ch.n
                        for sub in range(2):
                            fc = 2 * fp_ + sub
                            for rc in range(2):
                                so = rc
                                xq = sq[9] if so == 0 else sq[3]
                                SY.dma(xq, xres[so][:],
                                       xT_all[fc * 128:(fc + 1) * 128,
                                              rc * 512:(rc + 1) * 512],
                                       waits=[(DV.ch, x_done[so])])
                                tp = (2 * sub + rc) % 2
                                ev_ac = AC.op(
                                    "activation", t1[tp][:],
                                    ps_a[2 * sub + rc][:], AF.Identity,
                                    bias=bproj[:, fc:fc + 1], scale=1.0,
                                    waits=[(PE.ch, mm_done),
                                           (DV.ch, t_free[tp])])
                                ev_ac = AC.op(
                                    "activation", t2[tp][:],
                                    ps_b[2 * sub + rc][:], AF.Identity,
                                    bias=maskedb[:, fc:fc + 1],
                                    scale=rs_sb[:, fc:fc + 1])
                                ev_dv = DV.op(
                                    "tensor_tensor", t3[:], t2[tp][:],
                                    qselb[:, rc * 512:(rc + 1) * 512],
                                    ALU.mult,
                                    waits=[(AC.ch, ev_ac)])
                                x_done[so] = DV.op(
                                    "tensor_tensor", t1[tp][:], t1[tp][:],
                                    xres[so][:],
                                    ALU.add, waits=[(xq, xq.n)])
                                t_free[tp] = ev_dv = DV.op(
                                    "tensor_tensor", osb[so][:], t1[tp][:],
                                    t3[:],
                                    ALU.add, waits=[(gq[4 + so], o_done[so])])
                                PO.dma(gq[4 + so],
                                       outT[fc * 128:(fc + 1) * 128,
                                            rc * 512:(rc + 1) * 512],
                                       osb[so][:], waits=[(DV.ch, DV.ch.n)])
                                o_done[so] = gq[4 + so].n
                    PO.e.wait_ge(gq[4].h, gq[4].n)
                    PO.e.wait_ge(gq[5].h, gq[5].n)

        block = es.enter_context(nc.Block())

        @block.gpsimd
        def _(_g):
            emit_all()

    return nc


def kernel(L0_attn_hidden_states, L0_attn_w_attn, L0_attn_b_attn,
           L0_attn_w_proj, L0_attn_b_proj, L0_ffn_weight, L0_ffn_bias,
           L0_ffn_centroids, L0_ffn_assignments):
    x = np.ascontiguousarray(np.asarray(L0_attn_hidden_states, np.float32))
    w_attn = np.ascontiguousarray(np.asarray(L0_attn_w_attn, np.float32))
    b_attn = np.asarray(L0_attn_b_attn, np.float32)
    w_proj = np.ascontiguousarray(np.asarray(L0_attn_w_proj, np.float32))
    b_proj = np.asarray(L0_attn_b_proj, np.float32)
    ffn_w = np.asarray(L0_ffn_weight, np.float32)
    ffn_b = np.asarray(L0_ffn_bias, np.float32)
    cent = np.asarray(L0_ffn_centroids, np.float32)
    assign = np.asarray(L0_ffn_assignments, np.int32)

    B = x.shape[0]
    b_attn_t = np.ascontiguousarray(b_attn.reshape(96, 128).T)
    b_proj_t = np.ascontiguousarray(b_proj.reshape(32, 128).T)
    ffn_b_t = np.ascontiguousarray(ffn_b.reshape(32, 128).T)
    ffn_wT = np.ascontiguousarray(ffn_w.T)
    centT = np.ascontiguousarray(cent.T)
    assign_row = np.ascontiguousarray(assign.reshape(1, DM))
    ident = np.eye(128, dtype=np.float32).astype(ml_dtypes.bfloat16)

    bts = []
    for p in range(2):
        qglob = ((2 * (np.arange(OWN) // 128) + p) * 128
                 + (np.arange(OWN) % 128))
        kglob = np.arange(NKT * 128)
        rel = kglob[:, None].astype(np.float64) - qglob[None, :]
        bt = np.where(rel <= 0, rel, NEGBIG).astype(np.float32)
        bts.append(np.ascontiguousarray(bt.reshape(NKT, 128, OWN)))

    bt2s, basecols = [], []
    for p in range(2):
        qcg = [(8 * hf + 3 + p) * 128 + 64 for hf in range(2)]
        bt2 = np.empty((NKT, 128, 128), np.float32)
        base = np.empty((128, 32), np.float32)
        for kt in range(NKT):
            hf = 0 if kt < 8 else 1
            ie = kt // 2
            ki = np.arange(128)
            qe = np.arange(128)
            kglob = kt * 128 + ki
            qglob = (2 * ie + p) * 128 + qe
            val = (kglob[:, None] - qcg[hf]).astype(np.float64)
            mask = kglob[:, None] <= qglob[None, :]
            bt2[kt] = np.where(mask, val, NEGBIG).astype(np.float32)
        for hf in range(2):
            for kt in range(NKT):
                base[:, hf * 16 + kt] = (kt * 128 + np.arange(128)
                                         - qcg[hf]).astype(np.float32)
        bt2s.append(np.ascontiguousarray(bt2))
        basecols.append(np.ascontiguousarray(base))

    x_flat = x.reshape(B * SEQ, DM)
    in_maps = []
    for c in range(8):
        b, p = c // 2, c % 2
        xb = x[b]
        own_rows = xb.reshape(16, 128, DM)[p::2].reshape(OWN, DM)
        own_idx = (np.arange(16)[p::2][:, None] * 128
                   + np.arange(128)[None, :]).reshape(-1) + b * SEQ
        rest_idx = np.setdiff1d(np.arange(B * SEQ), own_idx)
        order = np.concatenate([own_idx, rest_idx])
        xT = np.ascontiguousarray(x_flat[order].T)
        in_maps.append({
            "x_full": xb,
            "x_own": np.ascontiguousarray(own_rows),
            "xT_all": xT,
            "w_attn": w_attn,
            "b_attn_t": b_attn_t,
            "w_proj": w_proj,
            "b_proj_t": b_proj_t,
            "ffn_wT": ffn_wT,
            "ffn_b_t": ffn_b_t,
            "assign_row": assign_row,
            "centT": centT,
            "bt_in": bts[p],
            "bt2e_in": bt2s[p],
            "basecol_in": basecols[p],
            "ident_in": ident,
        })

    nc = build_program()
    res = run_bass_kernel_spmd(nc, in_maps, core_ids=list(range(8)))

    out = np.empty((B, SEQ, DM), np.float32)
    for c in range(8):
        b, p = c // 2, c % 2
        oT = np.asarray(res.results[c]["outT"])
        rows = np.ascontiguousarray(oT.T).reshape(NQT, 128, DM)
        out.reshape(B, 16, 128, DM)[b, p::2] = rows
    return out


# version 13
# speedup vs baseline: 1.1586x; 1.1586x over previous
import contextlib
import math

import numpy as np
import ml_dtypes

import concourse.bass as bass
import concourse.mybir as mybir
from concourse.bass_utils import run_bass_kernel_spmd

F32 = mybir.dt.float32
BF16 = mybir.dt.bfloat16
I32 = mybir.dt.int32
AF = mybir.ActivationFunctionType
ALU = mybir.AluOpType

SEQ = 2048
DM = 4096
NKC = 32
NH = 32
HD = 128
OWN = 1024
NQT = 8
NKT = 16
NTOK = 8192
NCL = 64
THR = 0.01
TEMP_INV = 10.0
NEGBIG = -1.0e9


def _alibi_slopes(n_heads: int) -> np.ndarray:
    cp2 = 2 ** math.floor(math.log2(n_heads))
    base = 2 ** (-(2 ** (-(math.log2(cp2) - 3))))
    slopes = base ** np.arange(1, cp2 + 1, dtype=np.float64)
    if cp2 != n_heads:
        extra_base = 2 ** (-(2 ** (-(math.log2(2 * cp2) - 3))))
        n_rem = min(cp2, n_heads - cp2)
        extra = extra_base ** np.arange(1, 1 + 2 * n_rem, 2, dtype=np.float64)
        slopes = np.concatenate([slopes, extra])
    return slopes.astype(np.float32)


SLOPES = [float(s) for s in _alibi_slopes(NH)]


class Ch:
    def __init__(self, es, nc, name):
        self.h = es.enter_context(nc.semaphore(name))
        self.n = 0

    def inc(self, instr, k=1):
        instr.then_inc(self.h, k)
        self.n += k
        return self.n


class Eng:
    def __init__(self, es, nc, eng, name, auto):
        self.e = eng
        self.ch = Ch(es, nc, name + "c")
        self.auto = auto
        self._waited = {}

    def wait(self, ch, val):
        if val <= 0:
            return
        key = id(ch)
        if self._waited.get(key, -1) >= val:
            return
        assert val <= ch.n, f"wait {val} > produced {ch.n}"
        self.e.wait_ge(ch.h, val)
        self._waited[key] = val

    def op(self, method, *args, waits=(), **kw):
        for (c, v) in waits:
            self.wait(c, v)
        if self.auto and self.ch.n > 0:
            self.wait(self.ch, self.ch.n)
        instr = getattr(self.e, method)(*args, **kw)
        self.ch.inc(instr, 1)
        return self.ch.n

    def raw(self, method, *args, waits=(), **kw):
        for (c, v) in waits:
            self.wait(c, v)
        return getattr(self.e, method)(*args, **kw)

    def mark(self, instr):
        return self.ch.inc(instr, 1)

    def cast(self, out, in_, waits=()):
        m = "copy" if isinstance(self.e, bass.BassScalarEngine) else "tensor_copy"
        return self.op(m, out, in_, waits=waits)

    def dma(self, ch, out, in_, waits=()):
        for (c, v) in waits:
            self.wait(c, v)
        instr = self.e.dma_start(out=out, in_=in_)
        return ch.inc(instr, 16)


def ceil_div(a, b):
    return (a + b - 1) // b


def build_program():
    nc = bass.Bass("TRN2", target_bir_lowering=False, debug=False)

    x_full = nc.dram_tensor("x_full", [SEQ, DM], F32, kind="ExternalInput").ap()
    x_own = nc.dram_tensor("x_own", [OWN, DM], F32, kind="ExternalInput").ap()
    xT_all = nc.dram_tensor("xT_all", [DM, NTOK], F32, kind="ExternalInput").ap()
    w_attn = nc.dram_tensor("w_attn", [DM, 3 * DM], F32, kind="ExternalInput").ap()
    b_attn_t = nc.dram_tensor("b_attn_t", [128, 96], F32, kind="ExternalInput").ap()
    w_proj = nc.dram_tensor("w_proj", [DM, DM], F32, kind="ExternalInput").ap()
    b_proj_t = nc.dram_tensor("b_proj_t", [128, 32], F32, kind="ExternalInput").ap()
    ffn_wT = nc.dram_tensor("ffn_wT", [DM, DM], F32, kind="ExternalInput").ap()
    ffn_b_t = nc.dram_tensor("ffn_b_t", [128, 32], F32, kind="ExternalInput").ap()
    assign_row = nc.dram_tensor("assign_row", [1, DM], I32, kind="ExternalInput").ap()
    centT = nc.dram_tensor("centT", [DM, NCL], F32, kind="ExternalInput").ap()
    bt_in = nc.dram_tensor("bt_in", [NKT, 128, OWN], F32, kind="ExternalInput").ap()
    bt2e_in = nc.dram_tensor("bt2e_in", [NKT, 128, 128], F32,
                             kind="ExternalInput").ap()
    basecol_in = nc.dram_tensor("basecol_in", [128, 32], F32,
                                kind="ExternalInput").ap()
    ident_in = nc.dram_tensor("ident_in", [128, 128], BF16, kind="ExternalInput").ap()

    outT = nc.dram_tensor("outT", [DM, OWN], F32, kind="ExternalOutput").ap()

    xn_own_st = nc.dram_tensor("xn_own_st", [NKC, 128, OWN], BF16).ap()
    k_st = nc.dram_tensor("k_st", [NH, 128, SEQ], BF16).ap()
    v_st = nc.dram_tensor("v_st", [NH, 128, SEQ], BF16).ap()
    q_st = nc.dram_tensor("q_st", [NH, 128, OWN], BF16).ap()
    ofm_st = nc.dram_tensor("ofm_st", [NKC, 128, OWN], BF16).ap()

    es = contextlib.ExitStack()
    with es:
        def sb(name, shape, dt):
            return es.enter_context(nc.sbuf_tensor(name, shape, dt))

        identbf = sb("identbf", [128, 128], BF16)
        identf = sb("identf", [128, 128], F32)
        onescol = sb("onescol", [128, 1], BF16)
        onesrow = sb("onesrow", [1, 128], BF16)
        ones64r = sb("ones64r", [1, 64], BF16)
        battn = sb("battn", [128, 96], F32)
        bq128 = sb("bq128", [128, 32], F32)
        bproj = sb("bproj", [128, 32], F32)
        ffnb = sb("ffnb", [128, 32], F32)
        maskedb = sb("maskedb", [128, 32], F32)
        rs_sb = sb("rs_sb", [128, 32], F32)
        qsel_sb = sb("qsel_sb", [128, 8], F32)
        qselb = sb("qselb", [128, OWN], F32)
        centbf = sb("centbf", [128, NKC * 65], BF16)
        negSbf = sb("negSbf", [1, 64], BF16)
        iota64 = sb("iota64", [64, 1], F32)
        selgt = sb("selgt", [64, 1], F32)
        selbf = sb("selbf", [64, 1], BF16)
        ormax = sb("ormax", [128, 64], F32)
        statc = sb("statc", [128, 8], F32)
        epsrow = sb("epsrow", [1, 1], F32)
        c_cent = sb("c_cent", [128, NCL], F32)

        PE = Eng(es, nc, nc.tensor, "pe", auto=False)
        DV = Eng(es, nc, nc.vector, "dv", auto=True)
        AC = Eng(es, nc, nc.scalar, "ac", auto=True)
        PO = Eng(es, nc, nc.gpsimd, "po", auto=True)
        SY = Eng(es, nc, nc.sync, "sy", auto=False)

        sq = [Ch(es, nc, f"sq{i}") for i in range(10)]
        gq = [Ch(es, nc, f"gq{i}") for i in range(6)]
        wsq = [Ch(es, nc, f"wsq{i}") for i in range(8)]
        wsq2 = [Ch(es, nc, f"wsq2{i}") for i in range(8)]
        dsq = [Ch(es, nc, f"dsq{i}") for i in range(4)]
        posq = [Ch(es, nc, f"posq{i}") for i in range(8)]
        acq = [Ch(es, nc, f"acq{i}") for i in range(3)]
        aq2 = [Ch(es, nc, f"aq2{i}") for i in range(6)]

        cv = centbf[:].rearrange("p (k c) -> p k c", k=NKC)

        def barrier():
            for e in (DV, AC, PO):
                e.op("nop")
            marks = [(e.ch, e.ch.n) for e in (PE, DV, AC, PO)]
            qmarks = [(c, c.n) for c in sq + gq if c.n > 0]
            for e in (PE, DV, AC, PO, SY):
                for (c, v) in marks:
                    if c is not e.ch:
                        e.wait(c, v)
                for (c, v) in qmarks:
                    e.wait(c, v)

        def emit_all():
            SY.dma(sq[0], identbf[:], ident_in)
            SY.dma(sq[0], battn[:], b_attn_t)
            SY.dma(sq[0], bproj[:], b_proj_t)
            SY.dma(sq[0], ffnb[:], ffn_b_t)
            DV.op("memset", onescol[:], 1.0)
            DV.op("memset", onesrow[:], 1.0)
            DV.op("memset", ones64r[:], 1.0)
            PO.op("iota", iota64[:], [[1, 1]], channel_multiplier=1,
                  allow_small_or_imprecise_dtypes=True)
            DV.op("memset", cv[:, :, 64:65], 1.0 / DM)
            DV.op("memset", ormax[:], 0.0)
            DV.op("memset", epsrow[:], 1e-5)
            DV.op("tensor_copy", identf[:], identbf[:], waits=[(sq[0], sq[0].n)])
            DV.op("tensor_scalar", bq128[:], battn[:, 0:32], 1.0 / 128.0, None,
                  ALU.mult)
            with contextlib.ExitStack() as e0c:
                centstage = e0c.enter_context(
                    nc.sbuf_tensor("centstage", [128, NKC * 64], F32))
                csv = centstage[:].rearrange("p (k c) -> p k c", k=NKC)
                SY.dma(sq[1], csv,
                       centT.rearrange("(k p) c -> p k c", p=128))
                DV.op("tensor_copy",
                      cv[:, :, 0:64], csv, waits=[(sq[1], sq[1].n)])
            with contextlib.ExitStack() as e0:
                ps_s64 = e0.enter_context(nc.psum_tensor("ps_s64", [1, 64], F32))
                ins = None
                for kc in range(NKC):
                    ins = PE.raw("matmul", ps_s64[:], onescol[:],
                                 cv[:, kc, 0:64],
                                 start=(kc == 0), stop=(kc == NKC - 1),
                                 waits=[(DV.ch, DV.ch.n)] if kc == 0 else ())
                mS = PE.mark(ins)
                DV.op("tensor_scalar", negSbf[:], ps_s64[:], -1.0, None,
                      ALU.mult, waits=[(PE.ch, mS)])
                barrier()

            with contextlib.ExitStack() as e1:
                xn_fm = e1.enter_context(
                    nc.sbuf_tensor("xn_fm", [128, NKC * SEQ], BF16))
                xnv = xn_fm[:].rearrange("p (k t) -> p k t", k=NKC)
                e1a = contextlib.ExitStack()
                xt = [e1a.enter_context(nc.sbuf_tensor(f"p1x{s}", [128, DM], F32))
                      for s in range(2)]
                dummy = e1a.enter_context(nc.sbuf_tensor("p1dum", [128, DM], BF16))
                xntm = [e1a.enter_context(nc.sbuf_tensor(f"p1n{s}", [128, DM], BF16))
                        for s in range(2)]
                fmst = [e1a.enter_context(nc.sbuf_tensor(f"p1f{s}", [128, 512], BF16))
                        for s in range(2)]
                ps_t = [e1a.enter_context(nc.psum_tensor(f"p1t{s}", [128, 512], BF16))
                        for s in range(2)]
                x_done = [0, 0]
                xd_dv = [0, 0]
                n_done = [0, 0]
                f_done = [0, 0]
                t_done = [0, 0]

                for t in range(24):
                    own = t >= 16
                    s = t % 2
                    src = (x_own[(t - 16) * 128:(t - 15) * 128, :] if own
                           else x_full[t * 128:(t + 1) * 128, :])
                    wts = [(AC.ch, x_done[s]), (DV.ch, xd_dv[s])]
                    SY.dma(sq[1 + s], xt[s][:, 0:1024], src[:, 0:1024],
                           waits=wts)
                    SY.dma(sq[1 + s], xt[s][:, 1024:2048], src[:, 1024:2048])
                    AC.dma(acq[s], xt[s][:, 2048:3072], src[:, 2048:3072],
                           waits=wts)
                    AC.dma(acq[s], xt[s][:, 3072:4096], src[:, 3072:4096])
                    fills = [(sq[1 + s], sq[1 + s].n),
                             (acq[s], acq[s].n)]
                    xd_dv[s] = DV.op("reduce_sum", statc[:, 0:1], xt[s][:],
                                     axis=mybir.AxisListType.X, waits=fills)
                    AC.op("activation", dummy[:], xt[s][:], AF.Square,
                          accum_out=statc[:, 1:2], waits=fills)
                    DV.op("tensor_scalar", statc[:, 2:3], statc[:, 0:1],
                          1.0 / DM, None, ALU.mult)
                    DV.op("scalar_tensor_tensor", statc[:, 3:4], statc[:, 2:3],
                          -1.0, statc[:, 2:3], ALU.mult, ALU.mult)
                    DV.op("tensor_scalar", statc[:, 3:4], statc[:, 3:4],
                          1e-5, None, ALU.add)
                    AC.op("activation", statc[:, 4:5], statc[:, 1:2], AF.Sqrt,
                          bias=statc[:, 3:4], scale=1.0 / DM,
                          waits=[(DV.ch, DV.ch.n)])
                    DV.op("reciprocal", statc[:, 5:6], statc[:, 4:5],
                          waits=[(AC.ch, AC.ch.n)])
                    DV.op("scalar_tensor_tensor", statc[:, 6:7], statc[:, 2:3],
                          -1.0, statc[:, 5:6], ALU.mult, ALU.mult)
                    x_done[s] = AC.op(
                        "activation", xntm[s][:], xt[s][:], AF.Identity,
                        scale=statc[:, 5:6], bias=statc[:, 6:7],
                        waits=[(DV.ch, DV.ch.n), (PE.ch, n_done[s])])
                    for g in range(8):
                        ts = g % 2
                        ins = None
                        for j in range(4):
                            kc = 4 * g + j
                            ins = PE.raw(
                                "transpose", ps_t[ts][:, j * 128:(j + 1) * 128],
                                xntm[s][:, kc * 128:(kc + 1) * 128], identbf[:],
                                waits=([(AC.ch, x_done[s]), (DV.ch, t_done[ts])]
                                       if j == 0 else ()))
                        mpe = PE.mark(ins)
                        pv = ps_t[ts][:].rearrange("p (b c) -> p b c", b=4)
                        if own:
                            tq = t - 16
                            t_done[ts] = DV.op(
                                "tensor_copy", fmst[ts][:], ps_t[ts][:],
                                waits=[(PE.ch, mpe), (gq[ts], f_done[ts])])
                            fv = fmst[ts][:].rearrange("p (b c) -> p b c", b=4)
                            dst = xn_own_st[4 * g:4 * g + 4, :,
                                            tq * 128:(tq + 1) * 128]
                            dst = dst.rearrange("b p c -> p b c")
                            PO.dma(gq[ts], dst, fv,
                                   waits=[(DV.ch, t_done[ts])])
                            f_done[ts] = gq[ts].n
                        else:
                            t_done[ts] = DV.op(
                                "tensor_copy",
                                xnv[:, 4 * g:4 * g + 4, t * 128:(t + 1) * 128],
                                pv, waits=[(PE.ch, mpe)])
                    n_done[s] = PE.ch.n
                barrier()
                e1a.close()

                NSL = 6
                wt = [e1.enter_context(nc.sbuf_tensor(f"p2w{s}", [128, 256], F32))
                      for s in range(NSL)]
                wb = [e1.enter_context(nc.sbuf_tensor(f"p2b{s}", [128, 256], BF16))
                      for s in range(NSL)]
                kvout = [e1.enter_context(
                    nc.sbuf_tensor(f"p2o{s}", [128, SEQ], BF16)) for s in range(2)]
                ps_kv = [e1.enter_context(
                    nc.psum_tensor(f"p2p{i}", [128, 512], F32)) for i in range(8)]
                casters = [DV, PO, DV, PO, DV, PO]
                w_done = [(DV.ch, 0)] * NSL
                b_done = [0] * NSL
                o_done = [0, 0]
                ev_dv = 0
                ev_ac = 0
                for fp_ in range(NH):
                    col0 = DM + fp_ * 256
                    ins = None
                    for kc in range(NKC):
                        s = (fp_ * NKC + kc) % NSL
                        ce = casters[s]
                        qeng, qch = ((SY, wsq[s]) if s % 2 == 0
                                     else (AC, aq2[s]))
                        qeng.dma(qch, wt[s][:],
                                 w_attn[kc * 128:(kc + 1) * 128,
                                        col0:col0 + 256],
                                 waits=[w_done[s]])
                        w_done[s] = (ce.ch, ce.cast(
                            wb[s][:], wt[s][:],
                            waits=[(qch, qch.n), (PE.ch, b_done[s])]))
                        for sub in range(2):
                            for th in range(4):
                                ins = PE.raw(
                                    "matmul", ps_kv[4 * sub + th][:],
                                    wb[s][:, sub * 128:(sub + 1) * 128],
                                    xnv[:, kc, th * 512:(th + 1) * 512],
                                    start=(kc == 0), stop=(kc == NKC - 1),
                                    waits=([w_done[s], (DV.ch, ev_dv),
                                            (AC.ch, ev_ac)]
                                           if sub == 0 and th == 0 else ()))
                        b_done[s] = PE.mark(ins)
                    pe_g = PE.ch.n
                    for sub in range(2):
                        f = 2 * fp_ + sub
                        bias = battn[:, 32 + f:33 + f]
                        for th in range(4):
                            ev = (AC if th % 2 == 0 else DV)
                            if ev is AC:
                                ev_ac = AC.op(
                                    "activation",
                                    kvout[sub][:, th * 512:(th + 1) * 512],
                                    ps_kv[4 * sub + th][:], AF.Identity,
                                    bias=bias, scale=1.0,
                                    waits=[(PE.ch, pe_g),
                                           (gq[sub], o_done[sub])])
                            else:
                                ev_dv = DV.op(
                                    "tensor_scalar",
                                    kvout[sub][:, th * 512:(th + 1) * 512],
                                    ps_kv[4 * sub + th][:], 1.0, bias,
                                    ALU.mult, ALU.add,
                                    waits=[(PE.ch, pe_g),
                                           (gq[sub], o_done[sub])])
                        dst = (k_st[f] if f < NH else v_st[f - NH])
                        PO.dma(gq[sub], dst, kvout[sub][:],
                               waits=[(AC.ch, ev_ac), (DV.ch, ev_dv)])
                        o_done[sub] = gq[sub].n
                barrier()

            with contextlib.ExitStack() as e2:
                xno = e2.enter_context(
                    nc.sbuf_tensor("xno", [128, NKC * OWN], BF16))
                xnov = xno[:].rearrange("p (k t) -> p k t", k=NKC)
                for kc in range(NKC):
                    SY.dma(sq[6], xnov[:, kc, :], xn_own_st[kc])
                for e in (PE, DV, AC):
                    e.wait(sq[6], sq[6].n)

                with contextlib.ExitStack() as e23:
                    wt = [e23.enter_context(
                        nc.sbuf_tensor(f"qw{s}", [128, 256], F32))
                        for s in range(6)]
                    wb = [e23.enter_context(
                        nc.sbuf_tensor(f"qb{s}", [128, 256], BF16))
                        for s in range(6)]
                    qout = [e23.enter_context(
                        nc.sbuf_tensor(f"qo{s}", [128, OWN], BF16))
                        for s in range(2)]
                    xtt = [e23.enter_context(
                        nc.sbuf_tensor(f"d3x{s}", [128, 512], F32))
                        for s in range(6)]
                    xbb = [e23.enter_context(
                        nc.sbuf_tensor(f"d3b{s}", [128, 512], BF16))
                        for s in range(6)]
                    xsq = [e23.enter_context(
                        nc.sbuf_tensor(f"d3q{s}", [128, 512], BF16))
                        for s in range(6)]
                    rows = e23.enter_context(
                        nc.sbuf_tensor("d3r", [1, 6 * 512], F32))
                    rowbf = e23.enter_context(
                        nc.sbuf_tensor("d3rb", [1, 512], BF16))
                    rawsb = e23.enter_context(
                        nc.sbuf_tensor("d3w", [64, 512], BF16))
                    dtok = e23.enter_context(
                        nc.sbuf_tensor("d3t", [128, 256], BF16))
                    rstdc = e23.enter_context(
                        nc.sbuf_tensor("d3s", [128, 4], F32))
                    ed = e23.enter_context(nc.sbuf_tensor("d3e", [128, 64], F32))
                    gt = e23.enter_context(nc.sbuf_tensor("d3g", [128, 64], F32))
                    ohbf = e23.enter_context(
                        nc.sbuf_tensor("d3ob", [64, 512], BF16))
                    asgrow = e23.enter_context(
                        nc.sbuf_tensor("asgrow", [1, DM], BF16))
                    asg_i = e23.enter_context(
                        nc.sbuf_tensor("asg_i", [1, DM], I32))
                    rsrow = e23.enter_context(
                        nc.sbuf_tensor("rsrow", [1, DM], F32))
                    qrow = e23.enter_context(
                        nc.sbuf_tensor("qrow", [1, OWN], BF16))
                    SY.dma(sq[1], asg_i[:], assign_row)
                    DV.op("tensor_copy", asgrow[:], asg_i[:],
                          waits=[(sq[1], sq[1].n)])
                    ps_d2 = [e23.enter_context(
                        nc.psum_tensor(f"d3pd{i}", [128, 512], F32))
                        for i in range(2)]
                    ps_t3 = e23.enter_context(
                        nc.psum_tensor("d3pt", [128, 512], F32))
                    ps_t3b = e23.enter_context(
                        nc.psum_tensor("d3ptb", [128, 256], BF16))
                    e2bq = contextlib.ExitStack()
                    ps_q = [e2bq.enter_context(
                        nc.psum_tensor(f"qp{i}", [128, 512], F32))
                        for i in range(4)]

                    def p2b_gen():
                        casters = [DV, AC, DV, AC, DV, AC]
                        w_done = [(DV.ch, 0)] * 6
                        b_done = [0] * 6
                        o_done = [0, 0]
                        ev_done = [0, 0]
                        for fp_ in range(NH // 2):
                            ins = None
                            for kc in range(NKC):
                                s = (fp_ * NKC + kc) % 6
                                ce = casters[s]
                                qeng, qch = ((SY, wsq[s]) if s % 2 == 0
                                             else (AC, aq2[s]))
                                qeng.dma(qch, wt[s][:],
                                         w_attn[kc * 128:(kc + 1) * 128,
                                                fp_ * 256:fp_ * 256 + 256],
                                         waits=[w_done[s]])
                                w_done[s] = (ce.ch, ce.cast(
                                    wb[s][:], wt[s][:],
                                    waits=[(qch, qch.n),
                                           (PE.ch, b_done[s])]))
                                for sub in range(2):
                                    for th in range(2):
                                        ins = PE.raw(
                                            "matmul",
                                            ps_q[2 * sub + th][:],
                                            wb[s][:, sub * 128:
                                                 (sub + 1) * 128],
                                            xnov[:, kc,
                                                 th * 512:(th + 1) * 512],
                                            start=(kc == 0),
                                            stop=(kc == NKC - 1),
                                            waits=([w_done[s],
                                                    (AC.ch, ev_done[0])]
                                                   if sub == 0 and th == 0
                                                   else ()))
                                b_done[s] = PE.mark(ins)
                            pe_g = PE.ch.n
                            for sub in range(2):
                                f = 2 * fp_ + sub
                                for th in range(2):
                                    ev_done[0] = AC.op(
                                        "activation",
                                        qout[sub][:, th * 512:(th + 1) * 512],
                                        ps_q[2 * sub + th][:],
                                        AF.Identity,
                                        bias=bq128[:, f:f + 1],
                                        scale=1.0 / 128.0,
                                        waits=[(PE.ch, pe_g),
                                               (gq[sub], o_done[sub])])
                                PO.dma(gq[sub], q_st[f], qout[sub][:],
                                       waits=[(AC.ch, ev_done[0])])
                                o_done[sub] = gq[sub].n
                            yield

                    def p3_gen():
                        x_done = [0] * 6
                        b_done3 = [0] * 6
                        q_done3 = [(AC.ch, 0), (PO.ch, 0), (AC.ch, 0),
                                   (PO.ch, 0), (AC.ch, 0), (PO.ch, 0)]
                        psd_free_ac = [0, 0]
                        psd_free_dv = [0, 0]
                        for tg in range(16):
                            ps_d = ps_d2[tg % 2]
                            ins = None
                            for kc in range(NKC):
                                s = (tg * NKC + kc) % 6
                                sqe = AC if s % 2 == 0 else PO
                                xqe, xch = ((SY, dsq[s]) if s < 3
                                            else (AC, acq[s - 3]))
                                xqe.dma(xch, xtt[s][:],
                                        xT_all[kc * 128:(kc + 1) * 128,
                                               tg * 512:(tg + 1) * 512],
                                        waits=[(DV.ch, b_done3[s]),
                                               q_done3[s]])
                                b_done3[s] = DV.op(
                                    "tensor_copy", xbb[s][:], xtt[s][:],
                                    waits=[(xch, xch.n),
                                           (PE.ch, x_done[s])])
                                if sqe is AC:
                                    q_done3[s] = (AC.ch, AC.op(
                                        "activation", xsq[s][:], xtt[s][:],
                                        AF.Square,
                                        waits=[(xch, xch.n),
                                               (PE.ch, x_done[s])]))
                                else:
                                    q_done3[s] = (PO.ch, PO.op(
                                        "tensor_tensor", xsq[s][:], xtt[s][:],
                                        xtt[s][:], ALU.mult,
                                        waits=[(xch, xch.n),
                                               (PE.ch, x_done[s])]))
                                PE.raw("matmul", ps_d[0:64, :],
                                       cv[:, kc, 0:64], xbb[s][:],
                                       start=(kc == 0), stop=(kc == NKC - 1),
                                       waits=([(DV.ch, b_done3[s]),
                                               q_done3[s],
                                               (AC.ch, psd_free_ac[tg % 2]),
                                               (DV.ch, psd_free_dv[tg % 2])]
                                              if kc == 0 else
                                              [(DV.ch, b_done3[s]),
                                               q_done3[s]]))
                                PE.raw("matmul", ps_d[64:65, :], onescol[:],
                                       xbb[s][:],
                                       start=(kc == 0), stop=(kc == NKC - 1),
                                       tile_position=(0, 64),
                                       skip_group_check=True)
                                ins = PE.raw("matmul", ps_d[96:97, :],
                                             onescol[:], xsq[s][:],
                                             start=(kc == 0),
                                             stop=(kc == NKC - 1),
                                             tile_position=(0, 96),
                                             skip_group_check=True)
                                x_done[s] = PE.mark(ins)
                            mmm = PE.ch.n
                            DV.op("tensor_scalar", rows[:, 0:512],
                                  ps_d[64:65, :], 1.0 / DM, None, ALU.mult,
                                  waits=[(PE.ch, mmm)])
                            DV.op("tensor_copy", rows[:, 512:1024],
                                  ps_d[96:97, :])
                            DV.op("tensor_copy", rowbf[:], rows[:, 0:512])
                            DV.op("tensor_tensor", rows[:, 1024:1536],
                                  rows[:, 0:512], rows[:, 0:512], ALU.mult)
                            DV.op("scalar_tensor_tensor", rows[:, 1536:2048],
                                  rows[:, 512:1024], 1.0 / DM,
                                  rows[:, 1024:1536],
                                  ALU.mult, ALU.subtract)
                            AC.op("activation", rows[:, 2048:2560],
                                  rows[:, 1536:2048], AF.Sqrt, bias=epsrow[:],
                                  scale=1.0,
                                  waits=[(DV.ch, DV.ch.n)])
                            DV.op("reciprocal", rows[:, 2560:3072],
                                  rows[:, 2048:2560], waits=[(AC.ch, AC.ch.n)])
                            DV.op("tensor_scalar", rows[:, 2560:3072],
                                  rows[:, 2560:3072], TEMP_INV, None, ALU.mult)
                            ins = PE.raw("matmul", ps_d[0:64, :], negSbf[:],
                                         rowbf[:],
                                         start=False, stop=True,
                                         skip_group_check=True,
                                         waits=[(DV.ch, DV.ch.n)])
                            m2 = PE.mark(ins)
                            for j in range(4):
                                ins = PE.raw(
                                    "transpose", ps_t3[:, j:j + 1],
                                    rows[:, 2560 + j * 128:
                                         2560 + (j + 1) * 128],
                                    identf[0:1, 0:1],
                                    waits=[(DV.ch, DV.ch.n)] if j == 0 else ())
                            m3 = PE.mark(ins)
                            DV.op("tensor_copy", rstdc[:], ps_t3[:, 0:4],
                                  waits=[(PE.ch, m3)])
                            psd_free_ac[tg % 2] = AC.op(
                                "activation", rawsb[:], ps_d[0:64, :],
                                AF.Identity,
                                bias=0.0, scale=1.0, waits=[(PE.ch, m2)])
                            for j in range(4):
                                ins = PE.raw(
                                    "transpose", ps_t3b[:, j * 64:(j + 1) * 64],
                                    rawsb[:, j * 128:(j + 1) * 128],
                                    identbf[0:64, 0:64],
                                    waits=([(AC.ch, psd_free_ac[tg % 2]),
                                            (DV.ch, DV.ch.n)]
                                           if j == 0 else ()))
                            m4 = PE.mark(ins)
                            psd_free_dv[tg % 2] = DV.op(
                                "tensor_copy", dtok[:], ps_t3b[:],
                                waits=[(PE.ch, m4)])
                            for j in range(4):
                                AC.op("activation", ed[:],
                                      dtok[:, j * 64:(j + 1) * 64],
                                      AF.Exp, scale=rstdc[:, j:j + 1],
                                      accum_out=statc[:, 7:8],
                                      waits=[(DV.ch, DV.ch.n)])
                                DV.op("tensor_scalar", statc[:, 7:8],
                                      statc[:, 7:8],
                                      THR, None, ALU.mult,
                                      waits=[(AC.ch, AC.ch.n)])
                                DV.op("tensor_scalar", gt[:], ed[:],
                                      statc[:, 7:8], None, ALU.is_gt)
                                DV.op("tensor_tensor", ormax[:], ormax[:],
                                      gt[:], ALU.max)
                                if tg < 2:
                                    DV.op("reduce_max",
                                          qsel_sb[:, tg * 4 + j:
                                                  tg * 4 + j + 1],
                                          gt[:], axis=mybir.AxisListType.X)
                            yield

                    g2 = p2b_gen()
                    g3 = p3_gen()
                    done = object()
                    for i in range(16):
                        next(g2, done)
                        next(g3, done)
                    while next(g2, done) is not done:
                        pass
                    while next(g3, done) is not done:
                        pass
                    barrier()
                    e2bq.close()
                    ps_oh = e23.enter_context(
                        nc.psum_tensor("d3oh", [64, 512], F32))
                    ps_rs = e23.enter_context(
                        nc.psum_tensor("d3rs", [1, 512], F32))
                    ins = PE.raw("transpose", ps_t3[0:64, 0:128], ormax[:],
                                 identf[:], waits=[(DV.ch, DV.ch.n)])
                    mT = PE.mark(ins)
                    DV.op("reduce_max", selgt[:], ps_t3[0:64, 0:128],
                          axis=mybir.AxisListType.X, waits=[(PE.ch, mT)])
                    DV.op("tensor_scalar", selgt[:], selgt[:], 0.5, None,
                          ALU.is_gt)
                    DV.op("tensor_copy", selbf[:], selgt[:])
                    for c in range(8):
                        ins = PE.raw("matmul", ps_oh[:], ones64r[:],
                                     asgrow[:, c * 512:(c + 1) * 512],
                                     start=True, stop=True,
                                     waits=[(DV.ch, DV.ch.n)])
                        m5 = PE.mark(ins)
                        DV.op("tensor_scalar", ohbf[:], ps_oh[:], iota64[:],
                              None, ALU.is_equal, waits=[(PE.ch, m5)])
                        ins = PE.raw("matmul", ps_rs[:], selbf[:], ohbf[:],
                                     start=True, stop=True,
                                     waits=[(DV.ch, DV.ch.n)])
                        m6 = PE.mark(ins)
                        DV.op("tensor_copy", rsrow[:, c * 512:(c + 1) * 512],
                              ps_rs[:], waits=[(PE.ch, m6)])
                    for j in range(32):
                        ins = PE.raw(
                            "transpose", ps_t3[:, j:j + 1],
                            rsrow[:, j * 128:(j + 1) * 128], identf[0:1, 0:1],
                            waits=[(DV.ch, DV.ch.n)] if j == 0 else ())
                    m7 = PE.mark(ins)
                    DV.op("tensor_copy", rs_sb[:], ps_t3[:, 0:32],
                          waits=[(PE.ch, m7)])
                    DV.op("tensor_tensor", maskedb[:], ffnb[:], rs_sb[:],
                          ALU.mult)
                    for g in range(2):
                        ins = None
                        for j in range(4):
                            jj = g * 4 + j
                            ins = PE.raw(
                                "transpose", ps_t3[0:1, j * 128:(j + 1) * 128],
                                qsel_sb[:, jj:jj + 1], identf[:],
                                waits=[(DV.ch, DV.ch.n)] if j == 0 else ())
                        m8 = PE.mark(ins)
                        DV.op("tensor_copy", qrow[:, g * 512:(g + 1) * 512],
                              ps_t3[0:1, 0:512], waits=[(PE.ch, m8)])
                    for g in range(2):
                        ins = PE.raw("matmul", ps_t3[:],
                                     onesrow[:],
                                     qrow[:, g * 512:(g + 1) * 512],
                                     start=True, stop=True,
                                     waits=[(DV.ch, DV.ch.n)])
                        m9 = PE.mark(ins)
                        DV.op("tensor_copy", qselb[:, g * 512:(g + 1) * 512],
                              ps_t3[:], waits=[(PE.ch, m9)])
                    barrier()

                with contextlib.ExitStack() as e4:
                    btsb = e4.enter_context(
                        nc.sbuf_tensor("btsb", [128, NKT * OWN], F32))
                    btv = btsb[:].rearrange("p (k t) -> p k t", k=NKT)
                    bt2e = e4.enter_context(
                        nc.sbuf_tensor("bt2e", [128, NKT * 128], F32))
                    bt2v = bt2e[:].rearrange("p (k t) -> p k t", k=NKT)
                    basecol = e4.enter_context(
                        nc.sbuf_tensor("basecol", [128, 32], F32))
                    biasall = e4.enter_context(
                        nc.sbuf_tensor("biasall", [128, 32], F32))
                    qh = [e4.enter_context(
                        nc.sbuf_tensor(f"a4q{s}", [128, OWN], BF16))
                        for s in range(2)]
                    kh = [e4.enter_context(
                        nc.sbuf_tensor(f"a4k{s}", [128, SEQ], BF16))
                        for s in range(2)]
                    vh = [e4.enter_context(
                        nc.sbuf_tensor(f"a4v{s}", [128, SEQ], BF16))
                        for s in range(2)]
                    vaug = [e4.enter_context(
                        nc.sbuf_tensor(f"a4a{s}", [128, NKT * 129], BF16))
                        for s in range(2)]
                    ebuf = [e4.enter_context(
                        nc.sbuf_tensor(f"a4e{s}", [128, 512], BF16))
                        for s in range(4)]
                    tbuf = [e4.enter_context(
                        nc.sbuf_tensor(f"a4t{s}", [128, 512], F32))
                        for s in range(3)]
                    obuf = e4.enter_context(nc.sbuf_tensor("a4o", [128, OWN], BF16))
                    otsb = [e4.enter_context(
                        nc.sbuf_tensor(f"a4s{s}", [128, 512], BF16))
                        for s in range(2)]
                    recipb = e4.enter_context(nc.sbuf_tensor("a4r", [128, 8], F32))
                    ps_s = [e4.enter_context(
                        nc.psum_tensor(f"a4ps{i}", [128, 512], F32))
                        for i in range(3)]
                    ps_o4 = [e4.enter_context(
                        nc.psum_tensor(f"a4po{j}", [128, 512], F32))
                        for j in range(4)]
                    ps_tr0 = e4.enter_context(
                        nc.psum_tensor("a4pt", [128, 512], BF16))
                    for kt in range(NKT):
                        SY.dma(sq[6], btv[:, kt, :], bt_in[kt])
                        SY.dma(sq[6], bt2v[:, kt, :], bt2e_in[kt])
                    SY.dma(sq[6], basecol[:], basecol_in)
                    bt_m = sq[6].n
                    for s in range(2):
                        av = vaug[s][:].rearrange("p (k c) -> p k c", k=NKT)
                        DV.op("memset", av[:, :, 128:129], 1.0)
                    q_done = [0, 0]
                    k_done = [0, 0]
                    v_done = [0, 0]
                    a_done = [0, 0]
                    e_done = [0, 0, 0, 0]
                    t_done = [0, 0, 0]
                    s_done_dv = [0, 0, 0]
                    s_done_ac = [0, 0, 0]
                    bias_ready = 0
                    bias_free = 0
                    tr_done = [0, 0]
                    ot_done = [0, 0]
                    ps_o_free = 0
                    obuf_free = 0
                    pcnt = 0
                    pending_tail = [None]

                    def flush_tail():
                        if pending_tail[0] is not None:
                            pending_tail[0]()
                            pending_tail[0] = None

                    for h in range(NH):
                        s = h % 2
                        SY.dma(sq[4 + s], qh[s][:], q_st[h],
                               waits=[(PE.ch, q_done[s])])
                        SY.dma(sq[4 + s], kh[s][:], k_st[h],
                               waits=[(PE.ch, k_done[s])])
                        SY.dma(sq[4 + s], vh[s][:], v_st[h],
                               waits=[(PE.ch, v_done[s])])
                        fillm = sq[4 + s].n
                        av = vaug[s][:].rearrange("p (k c) -> p k c", k=NKT)
                        for g in range(4):
                            ins = None
                            for j in range(4):
                                kt = 4 * g + j
                                ins = PE.raw(
                                    "transpose",
                                    ps_tr0[:, j * 128:(j + 1) * 128],
                                    vh[s][:, kt * 128:(kt + 1) * 128],
                                    identbf[:],
                                    waits=([(sq[4 + s], fillm),
                                            (DV.ch, tr_done[0]),
                                            (PE.ch, a_done[s])]
                                           if j == 0 else ()))
                            mpe = PE.mark(ins)
                            pv = ps_tr0[:].rearrange("p (b c) -> p b c", b=4)
                            tr_done[0] = DV.op(
                                "tensor_copy", av[:, 4 * g:4 * g + 4, 0:128], pv,
                                waits=[(PE.ch, mpe)])
                        v_done[s] = PE.ch.n
                        vaug_ready = DV.ch.n
                        flush_tail()
                        steep = SLOPES[h] * 448.0 > 40.0
                        if not steep:
                            DV.op("tensor_scalar", biasall[:], basecol[:],
                                  SLOPES[h], None, ALU.mult,
                                  waits=[(sq[6], bt_m), (AC.ch, bias_free)])
                            bias_ready = DV.ch.n
                        for half in range(2):
                            ilo = 4 * half
                            ihi = ilo + 4
                            pend_av = []

                            def do_av(kt, exp_m, imin_h):
                                nonlocal pcnt
                                ins = None
                                for i in range(imin_h, ihi):
                                    ins = PE.raw(
                                        "matmul", ps_o4[i - ilo][:, 0:129],
                                        ebuf[kt % 4][:, (i - ilo) * 128:
                                                     (i - ilo + 1) * 128],
                                        av[:, kt, :],
                                        start=(kt == 0), stop=(kt == 2 * i + 1),
                                        skip_group_check=True,
                                        waits=(exp_m +
                                               [(DV.ch, vaug_ready),
                                                (AC.ch, ps_o_free)]
                                               if i == imin_h else ()))
                                return PE.mark(ins)

                            for kt in range(2 * ihi):
                                imin_h = max(kt // 2, ilo)
                                c0 = imin_h * 128
                                cw = ihi * 128 - c0
                                eb = kt % 4
                                p = pcnt % 3
                                pcnt += 1
                                ins = PE.raw(
                                    "matmul", ps_s[p][:, 0:cw],
                                    kh[s][:, kt * 128:(kt + 1) * 128],
                                    qh[s][:, c0:c0 + cw],
                                    start=True, stop=True,
                                    waits=[(sq[4 + s], fillm),
                                           (DV.ch, s_done_dv[p]),
                                           (AC.ch, s_done_ac[p])])
                                mS = PE.mark(ins)
                                el0 = c0 - ilo * 128
                                has_edge = kt >= 2 * ilo
                                exp_waits = []
                                if steep or not has_edge and steep:
                                    pass
                                if steep:
                                    s_done_dv[p] = DV.op(
                                        "scalar_tensor_tensor", tbuf[p][:, 0:cw],
                                        btv[:, kt, c0:c0 + cw], SLOPES[h],
                                        ps_s[p][:, 0:cw], ALU.mult, ALU.add,
                                        waits=[(PE.ch, mS), (sq[6], bt_m),
                                               (AC.ch, t_done[p])])
                                    t_done[p] = AC.op(
                                        "activation", ebuf[eb][:, el0:el0 + cw],
                                        tbuf[p][:, 0:cw], AF.Exp,
                                        waits=[(DV.ch, s_done_dv[p]),
                                               (PE.ch, e_done[eb])])
                                    s_done_ac[p] = t_done[p]
                                    exp_waits = [(AC.ch, t_done[p])]
                                else:
                                    bcol = biasall[:, half * 16 + kt:
                                                   half * 16 + kt + 1]
                                    if has_edge:
                                        s_done_dv[p] = DV.op(
                                            "scalar_tensor_tensor",
                                            tbuf[p][:, 0:128],
                                            bt2v[:, kt, :], SLOPES[h],
                                            ps_s[p][:, 0:128],
                                            ALU.mult, ALU.add,
                                            waits=[(PE.ch, mS), (sq[6], bt_m),
                                                   (AC.ch, t_done[p])])
                                        t_done[p] = AC.op(
                                            "activation",
                                            ebuf[eb][:, el0:el0 + 128],
                                            tbuf[p][:, 0:128], AF.Exp,
                                            waits=[(DV.ch, s_done_dv[p]),
                                                   (PE.ch, e_done[eb])])
                                        if cw > 128:
                                            t_done[p] = AC.op(
                                                "activation",
                                                ebuf[eb][:, el0 + 128:el0 + cw],
                                                ps_s[p][:, 128:cw], AF.Exp,
                                                bias=bcol, scale=1.0,
                                                waits=[(DV.ch, bias_ready)])
                                        s_done_ac[p] = t_done[p]
                                        exp_waits = [(AC.ch, t_done[p])]
                                    else:
                                        t_done[p] = AC.op(
                                            "activation",
                                            ebuf[eb][:, el0:el0 + cw],
                                            ps_s[p][:, 0:cw], AF.Exp,
                                            bias=bcol, scale=1.0,
                                            waits=[(PE.ch, mS),
                                                   (DV.ch, bias_ready),
                                                   (PE.ch, e_done[eb])])
                                        s_done_ac[p] = t_done[p]
                                        exp_waits = [(AC.ch, t_done[p])]
                                pend_av.append((kt, imin_h, exp_waits))
                                if len(pend_av) > 2:
                                    pkt, pimin, pw = pend_av.pop(0)
                                    e_done[pkt % 4] = a_done[s] = do_av(
                                        pkt, pw, pimin)
                            while pend_av:
                                pkt, pimin, pw = pend_av.pop(0)
                                e_done[pkt % 4] = a_done[s] = do_av(
                                    pkt, pw, pimin)
                            last_av = PE.ch.n
                            for j in range(4):
                                DV.op("reciprocal", recipb[:, ilo + j:ilo + j + 1],
                                      ps_o4[j][:, 128:129],
                                      waits=[(PE.ch, last_av),
                                             (AC.ch, ps_o_free)])
                            for j in range(4):
                                i = ilo + j
                                DV.op("tensor_scalar",
                                      obuf[:, i * 128:(i + 1) * 128],
                                      ps_o4[j][:, 0:128],
                                      recipb[:, i:i + 1], None, ALU.mult,
                                      waits=[(PE.ch, obuf_free)])
                            AC.op("nop", waits=[(DV.ch, DV.ch.n)])
                            ps_o_free = AC.ch.n
                        bias_free = AC.ch.n

                        def tail(h=h, s=s, pfree=ps_o_free):
                            nonlocal obuf_free
                            for g in range(2):
                                ts = g % 2
                                ins = None
                                for j in range(4):
                                    i = 4 * g + j
                                    ins = PE.raw(
                                        "transpose",
                                        ps_tr0[:, j * 128:(j + 1) * 128],
                                        obuf[:, i * 128:(i + 1) * 128],
                                        identbf[:],
                                        waits=([(AC.ch, pfree),
                                                (DV.ch, tr_done[0])]
                                               if j == 0 else ()))
                                mpe = PE.mark(ins)
                                tr_done[0] = DV.op(
                                    "tensor_copy", otsb[ts][:], ps_tr0[:],
                                    waits=[(PE.ch, mpe),
                                           (gq[2 + ts], ot_done[ts])])
                                PO.dma(gq[2 + ts],
                                       ofm_st[h, :, g * 512:(g + 1) * 512],
                                       otsb[ts][:],
                                       waits=[(DV.ch, tr_done[0]),
                                              (gq[2 + ts], gq[2 + ts].n)])
                                ot_done[ts] = gq[2 + ts].n
                            obuf_free = PE.ch.n
                            q_done[s] = k_done[s] = PE.ch.n
                        pending_tail[0] = tail
                    flush_tail()
                    barrier()

                with contextlib.ExitStack() as e5:
                    ofm = e5.enter_context(
                        nc.sbuf_tensor("ofm", [128, NKC * OWN], BF16))
                    ofv = ofm[:].rearrange("p (k t) -> p k t", k=NKC)
                    for kc in range(NKC):
                        SY.dma(sq[6], ofv[:, kc, :], ofm_st[kc])
                    ofm_m = sq[6].n
                    NSL = 6
                    wpt = [e5.enter_context(
                        nc.sbuf_tensor(f"f5p{s}", [128, 256], F32))
                        for s in range(NSL)]
                    wpb = [e5.enter_context(
                        nc.sbuf_tensor(f"f5pb{s}", [128, 256], BF16))
                        for s in range(NSL)]
                    wft = [e5.enter_context(
                        nc.sbuf_tensor(f"f5f{s}", [128, 256], F32))
                        for s in range(NSL)]
                    wfb = [e5.enter_context(
                        nc.sbuf_tensor(f"f5fb{s}", [128, 256], BF16))
                        for s in range(NSL)]
                    xres = [e5.enter_context(
                        nc.sbuf_tensor(f"f5x{s}", [128, 512], F32))
                        for s in range(2)]
                    t1 = [e5.enter_context(
                        nc.sbuf_tensor(f"f5t1{i}", [128, 512], F32))
                        for i in range(2)]
                    t2 = [e5.enter_context(
                        nc.sbuf_tensor(f"f5t2{i}", [128, 512], F32))
                        for i in range(2)]
                    t3 = e5.enter_context(nc.sbuf_tensor("f5t3", [128, 512], F32))
                    osb = [e5.enter_context(
                        nc.sbuf_tensor(f"f5o{s}", [128, 512], F32))
                        for s in range(2)]
                    ps_a = [e5.enter_context(
                        nc.psum_tensor(f"f5a{i}", [128, 512], F32))
                        for i in range(4)]
                    ps_b = [e5.enter_context(
                        nc.psum_tensor(f"f5b{i}", [128, 512], F32))
                        for i in range(4)]
                    casters = [DV, AC, DV, AC, DV, AC]
                    wp_done = [(DV.ch, 0)] * NSL
                    wf_done = [(DV.ch, 0)] * NSL
                    pb_done = [0] * NSL
                    fb_done = [0] * NSL
                    x_done = [0, 0]
                    o_done = [0, 0]
                    t_free = [0, 0]
                    ev_ac = 0
                    ev_dv = 0
                    for fp_ in range(NKC // 2):
                        for kc in range(NKC):
                            s = (fp_ * NKC + kc) % NSL
                            ce = casters[s]
                            SY.dma(wsq[s], wpt[s][:],
                                   w_proj[kc * 128:(kc + 1) * 128,
                                          fp_ * 256:(fp_ + 1) * 256],
                                   waits=[wp_done[s]])
                            wp_done[s] = (ce.ch, ce.cast(
                                wpb[s][:], wpt[s][:],
                                waits=[(wsq[s], wsq[s].n),
                                       (PE.ch, pb_done[s])]))
                            AC.dma(aq2[s], wft[s][:],
                                   ffn_wT[kc * 128:(kc + 1) * 128,
                                          fp_ * 256:(fp_ + 1) * 256],
                                   waits=[wf_done[s]])
                            wf_done[s] = (ce.ch, ce.cast(
                                wfb[s][:], wft[s][:],
                                waits=[(aq2[s], aq2[s].n),
                                       (PE.ch, fb_done[s])]))
                            ins = None
                            for sub in range(2):
                                for rc in range(2):
                                    ins = PE.raw(
                                        "matmul", ps_a[2 * sub + rc][:],
                                        wpb[s][:, sub * 128:(sub + 1) * 128],
                                        ofv[:, kc, rc * 512:(rc + 1) * 512],
                                        start=(kc == 0),
                                        stop=(kc == NKC - 1),
                                        waits=([wp_done[s],
                                                (AC.ch, ev_ac),
                                                (sq[6], ofm_m)]
                                               if sub == 0 and rc == 0
                                               else ()))
                            pb_done[s] = PE.mark(ins)
                            for sub in range(2):
                                for rc in range(2):
                                    ins = PE.raw(
                                        "matmul", ps_b[2 * sub + rc][:],
                                        wfb[s][:, sub * 128:(sub + 1) * 128],
                                        xnov[:, kc, rc * 512:(rc + 1) * 512],
                                        start=(kc == 0),
                                        stop=(kc == NKC - 1),
                                        waits=([wf_done[s]]
                                               if sub == 0 and rc == 0
                                               else ()))
                            fb_done[s] = PE.mark(ins)
                        mm_done = PE.ch.n
                        for sub in range(2):
                            fc = 2 * fp_ + sub
                            for rc in range(2):
                                so = rc
                                xq = sq[9] if so == 0 else sq[3]
                                SY.dma(xq, xres[so][:],
                                       xT_all[fc * 128:(fc + 1) * 128,
                                              rc * 512:(rc + 1) * 512],
                                       waits=[(DV.ch, x_done[so])])
                                tp = (2 * sub + rc) % 2
                                ev_ac = AC.op(
                                    "activation", t1[tp][:],
                                    ps_a[2 * sub + rc][:], AF.Identity,
                                    bias=bproj[:, fc:fc + 1], scale=1.0,
                                    waits=[(PE.ch, mm_done),
                                           (DV.ch, t_free[tp])])
                                ev_ac = AC.op(
                                    "activation", t2[tp][:],
                                    ps_b[2 * sub + rc][:], AF.Identity,
                                    bias=maskedb[:, fc:fc + 1],
                                    scale=rs_sb[:, fc:fc + 1])
                                ev_dv = DV.op(
                                    "tensor_tensor", t3[:], t2[tp][:],
                                    qselb[:, rc * 512:(rc + 1) * 512],
                                    ALU.mult,
                                    waits=[(AC.ch, ev_ac)])
                                x_done[so] = DV.op(
                                    "tensor_tensor", t1[tp][:], t1[tp][:],
                                    xres[so][:],
                                    ALU.add, waits=[(xq, xq.n)])
                                t_free[tp] = ev_dv = DV.op(
                                    "tensor_tensor", osb[so][:], t1[tp][:],
                                    t3[:],
                                    ALU.add, waits=[(gq[4 + so], o_done[so])])
                                PO.dma(gq[4 + so],
                                       outT[fc * 128:(fc + 1) * 128,
                                            rc * 512:(rc + 1) * 512],
                                       osb[so][:], waits=[(DV.ch, DV.ch.n)])
                                o_done[so] = gq[4 + so].n
                    PO.e.wait_ge(gq[4].h, gq[4].n)
                    PO.e.wait_ge(gq[5].h, gq[5].n)

        block = es.enter_context(nc.Block())

        @block.gpsimd
        def _(_g):
            emit_all()

    return nc


def kernel(L0_attn_hidden_states, L0_attn_w_attn, L0_attn_b_attn,
           L0_attn_w_proj, L0_attn_b_proj, L0_ffn_weight, L0_ffn_bias,
           L0_ffn_centroids, L0_ffn_assignments):
    x = np.ascontiguousarray(np.asarray(L0_attn_hidden_states, np.float32))
    w_attn = np.ascontiguousarray(np.asarray(L0_attn_w_attn, np.float32))
    b_attn = np.asarray(L0_attn_b_attn, np.float32)
    w_proj = np.ascontiguousarray(np.asarray(L0_attn_w_proj, np.float32))
    b_proj = np.asarray(L0_attn_b_proj, np.float32)
    ffn_w = np.asarray(L0_ffn_weight, np.float32)
    ffn_b = np.asarray(L0_ffn_bias, np.float32)
    cent = np.asarray(L0_ffn_centroids, np.float32)
    assign = np.asarray(L0_ffn_assignments, np.int32)

    B = x.shape[0]
    b_attn_t = np.ascontiguousarray(b_attn.reshape(96, 128).T)
    b_proj_t = np.ascontiguousarray(b_proj.reshape(32, 128).T)
    ffn_b_t = np.ascontiguousarray(ffn_b.reshape(32, 128).T)
    ffn_wT = np.ascontiguousarray(ffn_w.T)
    centT = np.ascontiguousarray(cent.T)
    assign_row = np.ascontiguousarray(assign.reshape(1, DM))
    ident = np.eye(128, dtype=np.float32).astype(ml_dtypes.bfloat16)

    bts = []
    for p in range(2):
        qglob = ((2 * (np.arange(OWN) // 128) + p) * 128
                 + (np.arange(OWN) % 128))
        kglob = np.arange(NKT * 128)
        rel = kglob[:, None].astype(np.float64) - qglob[None, :]
        bt = np.where(rel <= 0, rel, NEGBIG).astype(np.float32)
        bts.append(np.ascontiguousarray(bt.reshape(NKT, 128, OWN)))

    bt2s, basecols = [], []
    for p in range(2):
        qcg = [(8 * hf + 3 + p) * 128 + 64 for hf in range(2)]
        bt2 = np.empty((NKT, 128, 128), np.float32)
        base = np.empty((128, 32), np.float32)
        for kt in range(NKT):
            hf = 0 if kt < 8 else 1
            ie = kt // 2
            ki = np.arange(128)
            qe = np.arange(128)
            kglob = kt * 128 + ki
            qglob = (2 * ie + p) * 128 + qe
            val = (kglob[:, None] - qcg[hf]).astype(np.float64)
            mask = kglob[:, None] <= qglob[None, :]
            bt2[kt] = np.where(mask, val, NEGBIG).astype(np.float32)
        for hf in range(2):
            for kt in range(NKT):
                base[:, hf * 16 + kt] = (kt * 128 + np.arange(128)
                                         - qcg[hf]).astype(np.float32)
        bt2s.append(np.ascontiguousarray(bt2))
        basecols.append(np.ascontiguousarray(base))

    x_flat = x.reshape(B * SEQ, DM)
    in_maps = []
    for c in range(8):
        b, p = c // 2, c % 2
        xb = x[b]
        own_rows = xb.reshape(16, 128, DM)[p::2].reshape(OWN, DM)
        own_idx = (np.arange(16)[p::2][:, None] * 128
                   + np.arange(128)[None, :]).reshape(-1) + b * SEQ
        rest_idx = np.setdiff1d(np.arange(B * SEQ), own_idx)
        order = np.concatenate([own_idx, rest_idx])
        xT = np.ascontiguousarray(x_flat[order].T)
        in_maps.append({
            "x_full": xb,
            "x_own": np.ascontiguousarray(own_rows),
            "xT_all": xT,
            "w_attn": w_attn,
            "b_attn_t": b_attn_t,
            "w_proj": w_proj,
            "b_proj_t": b_proj_t,
            "ffn_wT": ffn_wT,
            "ffn_b_t": ffn_b_t,
            "assign_row": assign_row,
            "centT": centT,
            "bt_in": bts[p],
            "bt2e_in": bt2s[p],
            "basecol_in": basecols[p],
            "ident_in": ident,
        })

    nc = build_program()
    res = run_bass_kernel_spmd(nc, in_maps, core_ids=list(range(8)))

    out = np.empty((B, SEQ, DM), np.float32)
    for c in range(8):
        b, p = c // 2, c % 2
        oT = np.asarray(res.results[c]["outT"])
        rows = np.ascontiguousarray(oT.T).reshape(NQT, 128, DM)
        out.reshape(B, 16, 128, DM)[b, p::2] = rows
    return out


# version 14 (final)
# speedup vs baseline: 1.2220x; 1.0547x over previous
import contextlib
import math

import numpy as np
import ml_dtypes

import concourse.bass as bass
import concourse.mybir as mybir
from concourse.bass_utils import run_bass_kernel_spmd

F32 = mybir.dt.float32
BF16 = mybir.dt.bfloat16
I32 = mybir.dt.int32
AF = mybir.ActivationFunctionType
ALU = mybir.AluOpType

SEQ = 2048
DM = 4096
NKC = 32
NH = 32
HD = 128
OWN = 1024
NQT = 8
NKT = 16
NTOK = 8192
NCL = 64
THR = 0.01
TEMP_INV = 10.0
NEGBIG = -1.0e9


def _alibi_slopes(n_heads: int) -> np.ndarray:
    cp2 = 2 ** math.floor(math.log2(n_heads))
    base = 2 ** (-(2 ** (-(math.log2(cp2) - 3))))
    slopes = base ** np.arange(1, cp2 + 1, dtype=np.float64)
    if cp2 != n_heads:
        extra_base = 2 ** (-(2 ** (-(math.log2(2 * cp2) - 3))))
        n_rem = min(cp2, n_heads - cp2)
        extra = extra_base ** np.arange(1, 1 + 2 * n_rem, 2, dtype=np.float64)
        slopes = np.concatenate([slopes, extra])
    return slopes.astype(np.float32)


SLOPES = [float(s) for s in _alibi_slopes(NH)]


class Ch:
    def __init__(self, es, nc, name):
        self.h = es.enter_context(nc.semaphore(name))
        self.n = 0

    def inc(self, instr, k=1):
        instr.then_inc(self.h, k)
        self.n += k
        return self.n


class Eng:
    def __init__(self, es, nc, eng, name, auto):
        self.e = eng
        self.ch = Ch(es, nc, name + "c")
        self.auto = auto
        self._waited = {}

    def wait(self, ch, val):
        if val <= 0:
            return
        key = id(ch)
        if self._waited.get(key, -1) >= val:
            return
        assert val <= ch.n, f"wait {val} > produced {ch.n}"
        self.e.wait_ge(ch.h, val)
        self._waited[key] = val

    def op(self, method, *args, waits=(), **kw):
        for (c, v) in waits:
            self.wait(c, v)
        if self.auto and self.ch.n > 0:
            self.wait(self.ch, self.ch.n)
        instr = getattr(self.e, method)(*args, **kw)
        self.ch.inc(instr, 1)
        return self.ch.n

    def raw(self, method, *args, waits=(), **kw):
        for (c, v) in waits:
            self.wait(c, v)
        return getattr(self.e, method)(*args, **kw)

    def mark(self, instr):
        return self.ch.inc(instr, 1)

    def cast(self, out, in_, waits=()):
        m = "copy" if isinstance(self.e, bass.BassScalarEngine) else "tensor_copy"
        return self.op(m, out, in_, waits=waits)

    def dma(self, ch, out, in_, waits=()):
        for (c, v) in waits:
            self.wait(c, v)
        instr = self.e.dma_start(out=out, in_=in_)
        return ch.inc(instr, 16)


def ceil_div(a, b):
    return (a + b - 1) // b


def build_program():
    nc = bass.Bass("TRN2", target_bir_lowering=False, debug=False)

    x_full = nc.dram_tensor("x_full", [SEQ, DM], F32, kind="ExternalInput").ap()
    x_own = nc.dram_tensor("x_own", [OWN, DM], F32, kind="ExternalInput").ap()
    xT_all = nc.dram_tensor("xT_all", [DM, NTOK], F32, kind="ExternalInput").ap()
    w_attn = nc.dram_tensor("w_attn", [DM, 3 * DM], F32, kind="ExternalInput").ap()
    b_attn_t = nc.dram_tensor("b_attn_t", [128, 96], F32, kind="ExternalInput").ap()
    w_proj = nc.dram_tensor("w_proj", [DM, DM], F32, kind="ExternalInput").ap()
    b_proj_t = nc.dram_tensor("b_proj_t", [128, 32], F32, kind="ExternalInput").ap()
    ffn_wT = nc.dram_tensor("ffn_wT", [DM, DM], F32, kind="ExternalInput").ap()
    ffn_b_t = nc.dram_tensor("ffn_b_t", [128, 32], F32, kind="ExternalInput").ap()
    assign_row = nc.dram_tensor("assign_row", [1, DM], I32, kind="ExternalInput").ap()
    centT = nc.dram_tensor("centT", [DM, NCL], F32, kind="ExternalInput").ap()
    bt_in = nc.dram_tensor("bt_in", [NKT, 128, OWN], F32, kind="ExternalInput").ap()
    bt2e_in = nc.dram_tensor("bt2e_in", [NKT, 128, 128], F32,
                             kind="ExternalInput").ap()
    basecol_in = nc.dram_tensor("basecol_in", [128, 32], F32,
                                kind="ExternalInput").ap()
    ident_in = nc.dram_tensor("ident_in", [128, 128], BF16, kind="ExternalInput").ap()

    outT = nc.dram_tensor("outT", [DM, OWN], F32, kind="ExternalOutput").ap()

    xn_own_st = nc.dram_tensor("xn_own_st", [NKC, 128, OWN], BF16).ap()
    k_st = nc.dram_tensor("k_st", [NH, 128, SEQ], BF16).ap()
    v_st = nc.dram_tensor("v_st", [NH, 128, SEQ], BF16).ap()
    q_st = nc.dram_tensor("q_st", [NH, 128, OWN], BF16).ap()
    ofm_st = nc.dram_tensor("ofm_st", [NKC, 128, OWN], BF16).ap()

    es = contextlib.ExitStack()
    with es:
        def sb(name, shape, dt):
            return es.enter_context(nc.sbuf_tensor(name, shape, dt))

        identbf = sb("identbf", [128, 128], BF16)
        identf = sb("identf", [128, 128], F32)
        onescol = sb("onescol", [128, 1], BF16)
        onesrow = sb("onesrow", [1, 128], BF16)
        ones64r = sb("ones64r", [1, 64], BF16)
        battn = sb("battn", [128, 96], F32)
        bq128 = sb("bq128", [128, 32], F32)
        bproj = sb("bproj", [128, 32], F32)
        ffnb = sb("ffnb", [128, 32], F32)
        maskedb = sb("maskedb", [128, 32], F32)
        rs_sb = sb("rs_sb", [128, 32], F32)
        qsel_sb = sb("qsel_sb", [128, 8], F32)
        qselb = sb("qselb", [128, OWN], F32)
        centbf = sb("centbf", [128, NKC * 65], BF16)
        negSbf = sb("negSbf", [1, 64], BF16)
        iota64 = sb("iota64", [64, 1], F32)
        selgt = sb("selgt", [64, 1], F32)
        selbf = sb("selbf", [64, 1], BF16)
        ormax = sb("ormax", [128, 64], F32)
        statc = sb("statc", [128, 8], F32)
        epsrow = sb("epsrow", [1, 1], F32)
        c_cent = sb("c_cent", [128, NCL], F32)

        PE = Eng(es, nc, nc.tensor, "pe", auto=False)
        DV = Eng(es, nc, nc.vector, "dv", auto=True)
        AC = Eng(es, nc, nc.scalar, "ac", auto=True)
        PO = Eng(es, nc, nc.gpsimd, "po", auto=True)
        SY = Eng(es, nc, nc.sync, "sy", auto=False)

        sq = [Ch(es, nc, f"sq{i}") for i in range(10)]
        gq = [Ch(es, nc, f"gq{i}") for i in range(6)]
        wsq = [Ch(es, nc, f"wsq{i}") for i in range(8)]
        wsq2 = [Ch(es, nc, f"wsq2{i}") for i in range(8)]
        dsq = [Ch(es, nc, f"dsq{i}") for i in range(4)]
        posq = [Ch(es, nc, f"posq{i}") for i in range(8)]
        acq = [Ch(es, nc, f"acq{i}") for i in range(3)]
        aq2 = [Ch(es, nc, f"aq2{i}") for i in range(6)]

        cv = centbf[:].rearrange("p (k c) -> p k c", k=NKC)

        def barrier():
            for e in (DV, AC, PO):
                e.op("nop")
            marks = [(e.ch, e.ch.n) for e in (PE, DV, AC, PO)]
            qmarks = [(c, c.n) for c in sq + gq if c.n > 0]
            for e in (PE, DV, AC, PO, SY):
                for (c, v) in marks:
                    if c is not e.ch:
                        e.wait(c, v)
                for (c, v) in qmarks:
                    e.wait(c, v)

        def emit_all():
            SY.dma(sq[0], identbf[:], ident_in)
            SY.dma(sq[0], battn[:], b_attn_t)
            SY.dma(sq[0], bproj[:], b_proj_t)
            SY.dma(sq[0], ffnb[:], ffn_b_t)
            DV.op("memset", onescol[:], 1.0)
            DV.op("memset", onesrow[:], 1.0)
            DV.op("memset", ones64r[:], 1.0)
            PO.op("iota", iota64[:], [[1, 1]], channel_multiplier=1,
                  allow_small_or_imprecise_dtypes=True)
            DV.op("memset", cv[:, :, 64:65], 1.0 / DM)
            DV.op("memset", ormax[:], 0.0)
            DV.op("memset", epsrow[:], 1e-5)
            DV.op("tensor_copy", identf[:], identbf[:], waits=[(sq[0], sq[0].n)])
            DV.op("tensor_scalar", bq128[:], battn[:, 0:32], 1.0 / 128.0, None,
                  ALU.mult)
            with contextlib.ExitStack() as e0c:
                centstage = e0c.enter_context(
                    nc.sbuf_tensor("centstage", [128, NKC * 64], F32))
                csv = centstage[:].rearrange("p (k c) -> p k c", k=NKC)
                SY.dma(sq[1], csv,
                       centT.rearrange("(k p) c -> p k c", p=128))
                DV.op("tensor_copy",
                      cv[:, :, 0:64], csv, waits=[(sq[1], sq[1].n)])
            with contextlib.ExitStack() as e0:
                ps_s64 = e0.enter_context(nc.psum_tensor("ps_s64", [1, 64], F32))
                ins = None
                for kc in range(NKC):
                    ins = PE.raw("matmul", ps_s64[:], onescol[:],
                                 cv[:, kc, 0:64],
                                 start=(kc == 0), stop=(kc == NKC - 1),
                                 waits=[(DV.ch, DV.ch.n)] if kc == 0 else ())
                mS = PE.mark(ins)
                DV.op("tensor_scalar", negSbf[:], ps_s64[:], -1.0, None,
                      ALU.mult, waits=[(PE.ch, mS)])
                barrier()

            with contextlib.ExitStack() as e1:
                xn_fm = e1.enter_context(
                    nc.sbuf_tensor("xn_fm", [128, NKC * SEQ], BF16))
                xnv = xn_fm[:].rearrange("p (k t) -> p k t", k=NKC)
                e1a = contextlib.ExitStack()
                xt = [e1a.enter_context(nc.sbuf_tensor(f"p1x{s}", [128, DM], F32))
                      for s in range(2)]
                dummy = e1a.enter_context(nc.sbuf_tensor("p1dum", [128, DM], BF16))
                xntm = [e1a.enter_context(nc.sbuf_tensor(f"p1n{s}", [128, DM], BF16))
                        for s in range(2)]
                fmst = [e1a.enter_context(nc.sbuf_tensor(f"p1f{s}", [128, 512], BF16))
                        for s in range(2)]
                ps_t = [e1a.enter_context(nc.psum_tensor(f"p1t{s}", [128, 512], BF16))
                        for s in range(2)]
                x_done = [0, 0]
                xd_dv = [0, 0]
                n_done = [0, 0]
                f_done = [0, 0]
                t_done = [0, 0]

                for t in range(24):
                    own = t >= 16
                    s = t % 2
                    src = (x_own[(t - 16) * 128:(t - 15) * 128, :] if own
                           else x_full[t * 128:(t + 1) * 128, :])
                    wts = [(AC.ch, x_done[s]), (DV.ch, xd_dv[s])]
                    SY.dma(sq[1 + s], xt[s][:, 0:1024], src[:, 0:1024],
                           waits=wts)
                    SY.dma(sq[1 + s], xt[s][:, 1024:2048], src[:, 1024:2048])
                    AC.dma(acq[s], xt[s][:, 2048:3072], src[:, 2048:3072],
                           waits=wts)
                    AC.dma(acq[s], xt[s][:, 3072:4096], src[:, 3072:4096])
                    fills = [(sq[1 + s], sq[1 + s].n),
                             (acq[s], acq[s].n)]
                    xd_dv[s] = DV.op("reduce_sum", statc[:, 0:1], xt[s][:],
                                     axis=mybir.AxisListType.X, waits=fills)
                    AC.op("activation", dummy[:], xt[s][:], AF.Square,
                          accum_out=statc[:, 1:2], waits=fills)
                    DV.op("tensor_scalar", statc[:, 2:3], statc[:, 0:1],
                          1.0 / DM, None, ALU.mult)
                    DV.op("scalar_tensor_tensor", statc[:, 3:4], statc[:, 2:3],
                          -1.0, statc[:, 2:3], ALU.mult, ALU.mult)
                    DV.op("tensor_scalar", statc[:, 3:4], statc[:, 3:4],
                          1e-5, None, ALU.add)
                    AC.op("activation", statc[:, 4:5], statc[:, 1:2], AF.Sqrt,
                          bias=statc[:, 3:4], scale=1.0 / DM,
                          waits=[(DV.ch, DV.ch.n)])
                    DV.op("reciprocal", statc[:, 5:6], statc[:, 4:5],
                          waits=[(AC.ch, AC.ch.n)])
                    DV.op("scalar_tensor_tensor", statc[:, 6:7], statc[:, 2:3],
                          -1.0, statc[:, 5:6], ALU.mult, ALU.mult)
                    x_done[s] = AC.op(
                        "activation", xntm[s][:], xt[s][:], AF.Identity,
                        scale=statc[:, 5:6], bias=statc[:, 6:7],
                        waits=[(DV.ch, DV.ch.n), (PE.ch, n_done[s])])
                    for g in range(8):
                        ts = g % 2
                        ins = None
                        for j in range(4):
                            kc = 4 * g + j
                            ins = PE.raw(
                                "transpose", ps_t[ts][:, j * 128:(j + 1) * 128],
                                xntm[s][:, kc * 128:(kc + 1) * 128], identbf[:],
                                waits=([(AC.ch, x_done[s]), (DV.ch, t_done[ts])]
                                       if j == 0 else ()))
                        mpe = PE.mark(ins)
                        pv = ps_t[ts][:].rearrange("p (b c) -> p b c", b=4)
                        if own:
                            tq = t - 16
                            t_done[ts] = DV.op(
                                "tensor_copy", fmst[ts][:], ps_t[ts][:],
                                waits=[(PE.ch, mpe), (gq[ts], f_done[ts])])
                            fv = fmst[ts][:].rearrange("p (b c) -> p b c", b=4)
                            dst = xn_own_st[4 * g:4 * g + 4, :,
                                            tq * 128:(tq + 1) * 128]
                            dst = dst.rearrange("b p c -> p b c")
                            PO.dma(gq[ts], dst, fv,
                                   waits=[(DV.ch, t_done[ts])])
                            f_done[ts] = gq[ts].n
                        else:
                            t_done[ts] = DV.op(
                                "tensor_copy",
                                xnv[:, 4 * g:4 * g + 4, t * 128:(t + 1) * 128],
                                pv, waits=[(PE.ch, mpe)])
                    n_done[s] = PE.ch.n
                barrier()
                e1a.close()

                NSL = 6
                wt = [e1.enter_context(nc.sbuf_tensor(f"p2w{s}", [128, 256], F32))
                      for s in range(NSL)]
                wb = [e1.enter_context(nc.sbuf_tensor(f"p2b{s}", [128, 256], BF16))
                      for s in range(NSL)]
                kvout = [e1.enter_context(
                    nc.sbuf_tensor(f"p2o{s}", [128, SEQ], BF16)) for s in range(2)]
                ps_kv = [e1.enter_context(
                    nc.psum_tensor(f"p2p{i}", [128, 512], F32)) for i in range(8)]
                casters = [DV, PO, DV, PO, DV, PO]
                w_done = [(DV.ch, 0)] * NSL
                b_done = [0] * NSL
                o_done = [0, 0]
                ev_dv = 0
                ev_ac = 0
                for fp_ in range(NH):
                    col0 = DM + fp_ * 256
                    ins = None
                    for kc in range(NKC):
                        s = (fp_ * NKC + kc) % NSL
                        ce = casters[s]
                        qeng, qch = ((SY, wsq[s]) if s % 2 == 0
                                     else (AC, aq2[s]))
                        qeng.dma(qch, wt[s][:],
                                 w_attn[kc * 128:(kc + 1) * 128,
                                        col0:col0 + 256],
                                 waits=[w_done[s]])
                        w_done[s] = (ce.ch, ce.cast(
                            wb[s][:], wt[s][:],
                            waits=[(qch, qch.n), (PE.ch, b_done[s])]))
                        for sub in range(2):
                            for th in range(4):
                                ins = PE.raw(
                                    "matmul", ps_kv[4 * sub + th][:],
                                    wb[s][:, sub * 128:(sub + 1) * 128],
                                    xnv[:, kc, th * 512:(th + 1) * 512],
                                    start=(kc == 0), stop=(kc == NKC - 1),
                                    waits=([w_done[s], (DV.ch, ev_dv),
                                            (AC.ch, ev_ac)]
                                           if sub == 0 and th == 0 else ()))
                        b_done[s] = PE.mark(ins)
                    pe_g = PE.ch.n
                    for sub in range(2):
                        f = 2 * fp_ + sub
                        bias = battn[:, 32 + f:33 + f]
                        for th in range(4):
                            ev = (AC if th % 2 == 0 else DV)
                            if ev is AC:
                                ev_ac = AC.op(
                                    "activation",
                                    kvout[sub][:, th * 512:(th + 1) * 512],
                                    ps_kv[4 * sub + th][:], AF.Identity,
                                    bias=bias, scale=1.0,
                                    waits=[(PE.ch, pe_g),
                                           (gq[sub], o_done[sub])])
                            else:
                                ev_dv = DV.op(
                                    "tensor_scalar",
                                    kvout[sub][:, th * 512:(th + 1) * 512],
                                    ps_kv[4 * sub + th][:], 1.0, bias,
                                    ALU.mult, ALU.add,
                                    waits=[(PE.ch, pe_g),
                                           (gq[sub], o_done[sub])])
                        dst = (k_st[f] if f < NH else v_st[f - NH])
                        PO.dma(gq[sub], dst, kvout[sub][:],
                               waits=[(AC.ch, ev_ac), (DV.ch, ev_dv)])
                        o_done[sub] = gq[sub].n
                barrier()

            with contextlib.ExitStack() as e2:
                xno = e2.enter_context(
                    nc.sbuf_tensor("xno", [128, NKC * OWN], BF16))
                xnov = xno[:].rearrange("p (k t) -> p k t", k=NKC)
                for kc in range(NKC):
                    SY.dma(sq[6], xnov[:, kc, :], xn_own_st[kc])
                for e in (PE, DV, AC):
                    e.wait(sq[6], sq[6].n)

                with contextlib.ExitStack() as e23:
                    wt = [e23.enter_context(
                        nc.sbuf_tensor(f"qw{s}", [128, 256], F32))
                        for s in range(6)]
                    wb = [e23.enter_context(
                        nc.sbuf_tensor(f"qb{s}", [128, 256], BF16))
                        for s in range(6)]
                    qout = [e23.enter_context(
                        nc.sbuf_tensor(f"qo{s}", [128, OWN], BF16))
                        for s in range(2)]
                    xtt = [e23.enter_context(
                        nc.sbuf_tensor(f"d3x{s}", [128, 512], F32))
                        for s in range(6)]
                    xbb = [e23.enter_context(
                        nc.sbuf_tensor(f"d3b{s}", [128, 512], BF16))
                        for s in range(6)]
                    xsq = [e23.enter_context(
                        nc.sbuf_tensor(f"d3q{s}", [128, 512], BF16))
                        for s in range(6)]
                    rows = e23.enter_context(
                        nc.sbuf_tensor("d3r", [1, 6 * 512], F32))
                    rowbf = e23.enter_context(
                        nc.sbuf_tensor("d3rb", [1, 512], BF16))
                    rawsb = e23.enter_context(
                        nc.sbuf_tensor("d3w", [64, 512], BF16))
                    dtok = e23.enter_context(
                        nc.sbuf_tensor("d3t", [128, 256], BF16))
                    rstdc = e23.enter_context(
                        nc.sbuf_tensor("d3s", [128, 4], F32))
                    ed = e23.enter_context(nc.sbuf_tensor("d3e", [128, 64], F32))
                    gt = e23.enter_context(nc.sbuf_tensor("d3g", [128, 64], F32))
                    ohbf = e23.enter_context(
                        nc.sbuf_tensor("d3ob", [64, 512], BF16))
                    asgrow = e23.enter_context(
                        nc.sbuf_tensor("asgrow", [1, DM], BF16))
                    asg_i = e23.enter_context(
                        nc.sbuf_tensor("asg_i", [1, DM], I32))
                    rsrow = e23.enter_context(
                        nc.sbuf_tensor("rsrow", [1, DM], F32))
                    qrow = e23.enter_context(
                        nc.sbuf_tensor("qrow", [1, OWN], BF16))
                    SY.dma(sq[1], asg_i[:], assign_row)
                    DV.op("tensor_copy", asgrow[:], asg_i[:],
                          waits=[(sq[1], sq[1].n)])
                    ps_d2 = [e23.enter_context(
                        nc.psum_tensor(f"d3pd{i}", [128, 512], F32))
                        for i in range(2)]
                    ps_t3 = e23.enter_context(
                        nc.psum_tensor("d3pt", [128, 512], F32))
                    ps_t3b = e23.enter_context(
                        nc.psum_tensor("d3ptb", [128, 256], BF16))
                    e2bq = contextlib.ExitStack()
                    ps_q = [e2bq.enter_context(
                        nc.psum_tensor(f"qp{i}", [128, 512], F32))
                        for i in range(4)]

                    def p2b_gen():
                        casters = [DV, AC, DV, AC, DV, AC]
                        w_done = [(DV.ch, 0)] * 6
                        b_done = [0] * 6
                        o_done = [0, 0]
                        ev_done = [0, 0]
                        for fp_ in range(NH // 2):
                            ins = None
                            for kc in range(NKC):
                                s = (fp_ * NKC + kc) % 6
                                ce = casters[s]
                                qeng, qch = ((SY, wsq[s]) if s % 2 == 0
                                             else (AC, aq2[s]))
                                qeng.dma(qch, wt[s][:],
                                         w_attn[kc * 128:(kc + 1) * 128,
                                                fp_ * 256:fp_ * 256 + 256],
                                         waits=[w_done[s]])
                                w_done[s] = (ce.ch, ce.cast(
                                    wb[s][:], wt[s][:],
                                    waits=[(qch, qch.n),
                                           (PE.ch, b_done[s])]))
                                for sub in range(2):
                                    for th in range(2):
                                        ins = PE.raw(
                                            "matmul",
                                            ps_q[2 * sub + th][:],
                                            wb[s][:, sub * 128:
                                                 (sub + 1) * 128],
                                            xnov[:, kc,
                                                 th * 512:(th + 1) * 512],
                                            start=(kc == 0),
                                            stop=(kc == NKC - 1),
                                            waits=([w_done[s],
                                                    (AC.ch, ev_done[0])]
                                                   if sub == 0 and th == 0
                                                   else ()))
                                b_done[s] = PE.mark(ins)
                            pe_g = PE.ch.n
                            for sub in range(2):
                                f = 2 * fp_ + sub
                                for th in range(2):
                                    ev_done[0] = AC.op(
                                        "activation",
                                        qout[sub][:, th * 512:(th + 1) * 512],
                                        ps_q[2 * sub + th][:],
                                        AF.Identity,
                                        bias=bq128[:, f:f + 1],
                                        scale=1.0 / 128.0,
                                        waits=[(PE.ch, pe_g),
                                               (gq[sub], o_done[sub])])
                                PO.dma(gq[sub], q_st[f], qout[sub][:],
                                       waits=[(AC.ch, ev_done[0])])
                                o_done[sub] = gq[sub].n
                            yield

                    def p3_gen():
                        x_done = [0] * 6
                        b_done3 = [0] * 6
                        q_done3 = [(AC.ch, 0), (PO.ch, 0), (AC.ch, 0),
                                   (PO.ch, 0), (AC.ch, 0), (PO.ch, 0)]
                        psd_free_ac = [0, 0]
                        psd_free_dv = [0, 0]
                        for tg in range(16):
                            ps_d = ps_d2[tg % 2]
                            ins = None
                            for kc in range(NKC):
                                s = (tg * NKC + kc) % 6
                                sqe = AC if s % 2 == 0 else PO
                                xqe, xch = ((SY, dsq[s]) if s < 3
                                            else (SY, [sq[7], sq[8],
                                                       sq[0]][s - 3]))
                                xqe.dma(xch, xtt[s][:],
                                        xT_all[kc * 128:(kc + 1) * 128,
                                               tg * 512:(tg + 1) * 512],
                                        waits=[(DV.ch, b_done3[s]),
                                               q_done3[s]])
                                b_done3[s] = DV.op(
                                    "tensor_copy", xbb[s][:], xtt[s][:],
                                    waits=[(xch, xch.n),
                                           (PE.ch, x_done[s])])
                                if sqe is AC:
                                    q_done3[s] = (AC.ch, AC.op(
                                        "activation", xsq[s][:], xtt[s][:],
                                        AF.Square,
                                        waits=[(xch, xch.n),
                                               (PE.ch, x_done[s])]))
                                else:
                                    q_done3[s] = (PO.ch, PO.op(
                                        "tensor_tensor", xsq[s][:], xtt[s][:],
                                        xtt[s][:], ALU.mult,
                                        waits=[(xch, xch.n),
                                               (PE.ch, x_done[s])]))
                                PE.raw("matmul", ps_d[0:64, :],
                                       cv[:, kc, 0:64], xbb[s][:],
                                       start=(kc == 0), stop=(kc == NKC - 1),
                                       waits=([(DV.ch, b_done3[s]),
                                               q_done3[s],
                                               (AC.ch, psd_free_ac[tg % 2]),
                                               (DV.ch, psd_free_dv[tg % 2])]
                                              if kc == 0 else
                                              [(DV.ch, b_done3[s]),
                                               q_done3[s]]))
                                PE.raw("matmul", ps_d[64:65, :], onescol[:],
                                       xbb[s][:],
                                       start=(kc == 0), stop=(kc == NKC - 1),
                                       tile_position=(0, 64),
                                       skip_group_check=True)
                                ins = PE.raw("matmul", ps_d[96:97, :],
                                             onescol[:], xsq[s][:],
                                             start=(kc == 0),
                                             stop=(kc == NKC - 1),
                                             tile_position=(0, 96),
                                             skip_group_check=True)
                                x_done[s] = PE.mark(ins)
                            mmm = PE.ch.n
                            DV.op("tensor_scalar", rows[:, 0:512],
                                  ps_d[64:65, :], 1.0 / DM, None, ALU.mult,
                                  waits=[(PE.ch, mmm)])
                            DV.op("tensor_copy", rows[:, 512:1024],
                                  ps_d[96:97, :])
                            DV.op("tensor_copy", rowbf[:], rows[:, 0:512])
                            DV.op("tensor_tensor", rows[:, 1024:1536],
                                  rows[:, 0:512], rows[:, 0:512], ALU.mult)
                            DV.op("scalar_tensor_tensor", rows[:, 1536:2048],
                                  rows[:, 512:1024], 1.0 / DM,
                                  rows[:, 1024:1536],
                                  ALU.mult, ALU.subtract)
                            AC.op("activation", rows[:, 2048:2560],
                                  rows[:, 1536:2048], AF.Sqrt, bias=epsrow[:],
                                  scale=1.0,
                                  waits=[(DV.ch, DV.ch.n)])
                            DV.op("reciprocal", rows[:, 2560:3072],
                                  rows[:, 2048:2560], waits=[(AC.ch, AC.ch.n)])
                            DV.op("tensor_scalar", rows[:, 2560:3072],
                                  rows[:, 2560:3072], TEMP_INV, None, ALU.mult)
                            ins = PE.raw("matmul", ps_d[0:64, :], negSbf[:],
                                         rowbf[:],
                                         start=False, stop=True,
                                         skip_group_check=True,
                                         waits=[(DV.ch, DV.ch.n)])
                            m2 = PE.mark(ins)
                            for j in range(4):
                                ins = PE.raw(
                                    "transpose", ps_t3[:, j:j + 1],
                                    rows[:, 2560 + j * 128:
                                         2560 + (j + 1) * 128],
                                    identf[0:1, 0:1],
                                    waits=[(DV.ch, DV.ch.n)] if j == 0 else ())
                            m3 = PE.mark(ins)
                            DV.op("tensor_copy", rstdc[:], ps_t3[:, 0:4],
                                  waits=[(PE.ch, m3)])
                            psd_free_ac[tg % 2] = AC.op(
                                "activation", rawsb[:], ps_d[0:64, :],
                                AF.Identity,
                                bias=0.0, scale=1.0, waits=[(PE.ch, m2)])
                            for j in range(4):
                                ins = PE.raw(
                                    "transpose", ps_t3b[:, j * 64:(j + 1) * 64],
                                    rawsb[:, j * 128:(j + 1) * 128],
                                    identbf[0:64, 0:64],
                                    waits=([(AC.ch, psd_free_ac[tg % 2]),
                                            (DV.ch, DV.ch.n)]
                                           if j == 0 else ()))
                            m4 = PE.mark(ins)
                            psd_free_dv[tg % 2] = DV.op(
                                "tensor_copy", dtok[:], ps_t3b[:],
                                waits=[(PE.ch, m4)])
                            for j in range(4):
                                AC.op("activation", ed[:],
                                      dtok[:, j * 64:(j + 1) * 64],
                                      AF.Exp, scale=rstdc[:, j:j + 1],
                                      accum_out=statc[:, 7:8],
                                      waits=[(DV.ch, DV.ch.n)])
                                DV.op("tensor_scalar", statc[:, 7:8],
                                      statc[:, 7:8],
                                      THR, None, ALU.mult,
                                      waits=[(AC.ch, AC.ch.n)])
                                DV.op("tensor_scalar", gt[:], ed[:],
                                      statc[:, 7:8], None, ALU.is_gt)
                                DV.op("tensor_tensor", ormax[:], ormax[:],
                                      gt[:], ALU.max)
                                if tg < 2:
                                    DV.op("reduce_max",
                                          qsel_sb[:, tg * 4 + j:
                                                  tg * 4 + j + 1],
                                          gt[:], axis=mybir.AxisListType.X)
                            yield

                    g2 = p2b_gen()
                    g3 = p3_gen()
                    done = object()
                    for i in range(16):
                        next(g2, done)
                        next(g3, done)
                    while next(g2, done) is not done:
                        pass
                    while next(g3, done) is not done:
                        pass
                    barrier()
                    e2bq.close()
                    ps_oh = e23.enter_context(
                        nc.psum_tensor("d3oh", [64, 512], F32))
                    ps_rs = e23.enter_context(
                        nc.psum_tensor("d3rs", [1, 512], F32))
                    ins = PE.raw("transpose", ps_t3[0:64, 0:128], ormax[:],
                                 identf[:], waits=[(DV.ch, DV.ch.n)])
                    mT = PE.mark(ins)
                    DV.op("reduce_max", selgt[:], ps_t3[0:64, 0:128],
                          axis=mybir.AxisListType.X, waits=[(PE.ch, mT)])
                    DV.op("tensor_scalar", selgt[:], selgt[:], 0.5, None,
                          ALU.is_gt)
                    DV.op("tensor_copy", selbf[:], selgt[:])
                    for c in range(8):
                        ins = PE.raw("matmul", ps_oh[:], ones64r[:],
                                     asgrow[:, c * 512:(c + 1) * 512],
                                     start=True, stop=True,
                                     waits=[(DV.ch, DV.ch.n)])
                        m5 = PE.mark(ins)
                        DV.op("tensor_scalar", ohbf[:], ps_oh[:], iota64[:],
                              None, ALU.is_equal, waits=[(PE.ch, m5)])
                        ins = PE.raw("matmul", ps_rs[:], selbf[:], ohbf[:],
                                     start=True, stop=True,
                                     waits=[(DV.ch, DV.ch.n)])
                        m6 = PE.mark(ins)
                        DV.op("tensor_copy", rsrow[:, c * 512:(c + 1) * 512],
                              ps_rs[:], waits=[(PE.ch, m6)])
                    for j in range(32):
                        ins = PE.raw(
                            "transpose", ps_t3[:, j:j + 1],
                            rsrow[:, j * 128:(j + 1) * 128], identf[0:1, 0:1],
                            waits=[(DV.ch, DV.ch.n)] if j == 0 else ())
                    m7 = PE.mark(ins)
                    DV.op("tensor_copy", rs_sb[:], ps_t3[:, 0:32],
                          waits=[(PE.ch, m7)])
                    DV.op("tensor_tensor", maskedb[:], ffnb[:], rs_sb[:],
                          ALU.mult)
                    for g in range(2):
                        ins = None
                        for j in range(4):
                            jj = g * 4 + j
                            ins = PE.raw(
                                "transpose", ps_t3[0:1, j * 128:(j + 1) * 128],
                                qsel_sb[:, jj:jj + 1], identf[:],
                                waits=[(DV.ch, DV.ch.n)] if j == 0 else ())
                        m8 = PE.mark(ins)
                        DV.op("tensor_copy", qrow[:, g * 512:(g + 1) * 512],
                              ps_t3[0:1, 0:512], waits=[(PE.ch, m8)])
                    for g in range(2):
                        ins = PE.raw("matmul", ps_t3[:],
                                     onesrow[:],
                                     qrow[:, g * 512:(g + 1) * 512],
                                     start=True, stop=True,
                                     waits=[(DV.ch, DV.ch.n)])
                        m9 = PE.mark(ins)
                        DV.op("tensor_copy", qselb[:, g * 512:(g + 1) * 512],
                              ps_t3[:], waits=[(PE.ch, m9)])
                    barrier()

                with contextlib.ExitStack() as e4:
                    btsb = e4.enter_context(
                        nc.sbuf_tensor("btsb", [128, NKT * OWN], F32))
                    btv = btsb[:].rearrange("p (k t) -> p k t", k=NKT)
                    bt2e = e4.enter_context(
                        nc.sbuf_tensor("bt2e", [128, NKT * 128], F32))
                    bt2v = bt2e[:].rearrange("p (k t) -> p k t", k=NKT)
                    basecol = e4.enter_context(
                        nc.sbuf_tensor("basecol", [128, 32], F32))
                    biasall = e4.enter_context(
                        nc.sbuf_tensor("biasall", [128, 32], F32))
                    qh = [e4.enter_context(
                        nc.sbuf_tensor(f"a4q{s}", [128, OWN], BF16))
                        for s in range(2)]
                    kh = [e4.enter_context(
                        nc.sbuf_tensor(f"a4k{s}", [128, SEQ], BF16))
                        for s in range(2)]
                    vh = [e4.enter_context(
                        nc.sbuf_tensor(f"a4v{s}", [128, SEQ], BF16))
                        for s in range(2)]
                    vaug = [e4.enter_context(
                        nc.sbuf_tensor(f"a4a{s}", [128, NKT * 129], BF16))
                        for s in range(2)]
                    ebuf = [e4.enter_context(
                        nc.sbuf_tensor(f"a4e{s}", [128, 512], BF16))
                        for s in range(4)]
                    tbuf = [e4.enter_context(
                        nc.sbuf_tensor(f"a4t{s}", [128, 512], F32))
                        for s in range(3)]
                    obuf = e4.enter_context(nc.sbuf_tensor("a4o", [128, OWN], BF16))
                    otsb = [e4.enter_context(
                        nc.sbuf_tensor(f"a4s{s}", [128, 512], BF16))
                        for s in range(2)]
                    recipb = e4.enter_context(nc.sbuf_tensor("a4r", [128, 8], F32))
                    ps_s = [e4.enter_context(
                        nc.psum_tensor(f"a4ps{i}", [128, 512], F32))
                        for i in range(3)]
                    ps_o4 = [e4.enter_context(
                        nc.psum_tensor(f"a4po{j}", [128, 512], F32))
                        for j in range(4)]
                    ps_tr0 = e4.enter_context(
                        nc.psum_tensor("a4pt", [128, 512], BF16))
                    for kt in range(NKT):
                        SY.dma(sq[6], btv[:, kt, :], bt_in[kt])
                        SY.dma(sq[6], bt2v[:, kt, :], bt2e_in[kt])
                    SY.dma(sq[6], basecol[:], basecol_in)
                    bt_m = sq[6].n
                    for s in range(2):
                        av = vaug[s][:].rearrange("p (k c) -> p k c", k=NKT)
                        DV.op("memset", av[:, :, 128:129], 1.0)
                    q_done = [0, 0]
                    k_done = [0, 0]
                    v_done = [0, 0]
                    a_done = [0, 0]
                    e_done = [0, 0, 0, 0]
                    t_done = [0, 0, 0]
                    s_done_dv = [0, 0, 0]
                    s_done_ac = [0, 0, 0]
                    bias_ready = 0
                    bias_free = 0
                    tr_done = [0, 0]
                    ot_done = [0, 0]
                    ps_o_free = 0
                    obuf_free = 0
                    pcnt = 0
                    pending_tail = [None]

                    def flush_tail():
                        if pending_tail[0] is not None:
                            pending_tail[0]()
                            pending_tail[0] = None

                    for h in range(NH):
                        s = h % 2
                        SY.dma(sq[4 + s], qh[s][:], q_st[h],
                               waits=[(PE.ch, q_done[s])])
                        SY.dma(sq[4 + s], kh[s][:], k_st[h],
                               waits=[(PE.ch, k_done[s])])
                        SY.dma(sq[4 + s], vh[s][:], v_st[h],
                               waits=[(PE.ch, v_done[s])])
                        fillm = sq[4 + s].n
                        av = vaug[s][:].rearrange("p (k c) -> p k c", k=NKT)
                        for g in range(4):
                            ins = None
                            for j in range(4):
                                kt = 4 * g + j
                                ins = PE.raw(
                                    "transpose",
                                    ps_tr0[:, j * 128:(j + 1) * 128],
                                    vh[s][:, kt * 128:(kt + 1) * 128],
                                    identbf[:],
                                    waits=([(sq[4 + s], fillm),
                                            (DV.ch, tr_done[0]),
                                            (PE.ch, a_done[s])]
                                           if j == 0 else ()))
                            mpe = PE.mark(ins)
                            pv = ps_tr0[:].rearrange("p (b c) -> p b c", b=4)
                            tr_done[0] = DV.op(
                                "tensor_copy", av[:, 4 * g:4 * g + 4, 0:128], pv,
                                waits=[(PE.ch, mpe)])
                        v_done[s] = PE.ch.n
                        vaug_ready = DV.ch.n
                        flush_tail()
                        steep = SLOPES[h] * 448.0 > 40.0
                        if not steep:
                            DV.op("tensor_scalar", biasall[:], basecol[:],
                                  SLOPES[h], None, ALU.mult,
                                  waits=[(sq[6], bt_m), (AC.ch, bias_free)])
                            bias_ready = DV.ch.n
                        for half in range(2):
                            ilo = 4 * half
                            ihi = ilo + 4
                            pend_av = []

                            def do_av(kt, exp_m, imin_h):
                                nonlocal pcnt
                                ins = None
                                for i in range(imin_h, ihi):
                                    ins = PE.raw(
                                        "matmul", ps_o4[i - ilo][:, 0:129],
                                        ebuf[kt % 4][:, (i - ilo) * 128:
                                                     (i - ilo + 1) * 128],
                                        av[:, kt, :],
                                        start=(kt == 0), stop=(kt == 2 * i + 1),
                                        skip_group_check=True,
                                        waits=(exp_m +
                                               [(DV.ch, vaug_ready),
                                                (AC.ch, ps_o_free)]
                                               if i == imin_h else ()))
                                return PE.mark(ins)

                            for kt in range(2 * ihi):
                                imin_h = max(kt // 2, ilo)
                                c0 = imin_h * 128
                                cw = ihi * 128 - c0
                                eb = kt % 4
                                p = pcnt % 3
                                pcnt += 1
                                ins = PE.raw(
                                    "matmul", ps_s[p][:, 0:cw],
                                    kh[s][:, kt * 128:(kt + 1) * 128],
                                    qh[s][:, c0:c0 + cw],
                                    start=True, stop=True,
                                    waits=[(sq[4 + s], fillm),
                                           (DV.ch, s_done_dv[p]),
                                           (AC.ch, s_done_ac[p])])
                                mS = PE.mark(ins)
                                el0 = c0 - ilo * 128
                                has_edge = kt >= 2 * ilo
                                exp_waits = []
                                if steep or not has_edge and steep:
                                    pass
                                if steep:
                                    s_done_dv[p] = DV.op(
                                        "scalar_tensor_tensor", tbuf[p][:, 0:cw],
                                        btv[:, kt, c0:c0 + cw], SLOPES[h],
                                        ps_s[p][:, 0:cw], ALU.mult, ALU.add,
                                        waits=[(PE.ch, mS), (sq[6], bt_m),
                                               (AC.ch, t_done[p])])
                                    t_done[p] = AC.op(
                                        "activation", ebuf[eb][:, el0:el0 + cw],
                                        tbuf[p][:, 0:cw], AF.Exp,
                                        waits=[(DV.ch, s_done_dv[p]),
                                               (PE.ch, e_done[eb])])
                                    s_done_ac[p] = t_done[p]
                                    exp_waits = [(AC.ch, t_done[p])]
                                else:
                                    bcol = biasall[:, half * 16 + kt:
                                                   half * 16 + kt + 1]
                                    if has_edge:
                                        s_done_dv[p] = DV.op(
                                            "scalar_tensor_tensor",
                                            tbuf[p][:, 0:128],
                                            bt2v[:, kt, :], SLOPES[h],
                                            ps_s[p][:, 0:128],
                                            ALU.mult, ALU.add,
                                            waits=[(PE.ch, mS), (sq[6], bt_m),
                                                   (AC.ch, t_done[p])])
                                        t_done[p] = AC.op(
                                            "activation",
                                            ebuf[eb][:, el0:el0 + 128],
                                            tbuf[p][:, 0:128], AF.Exp,
                                            waits=[(DV.ch, s_done_dv[p]),
                                                   (PE.ch, e_done[eb])])
                                        if cw > 128:
                                            t_done[p] = AC.op(
                                                "activation",
                                                ebuf[eb][:, el0 + 128:el0 + cw],
                                                ps_s[p][:, 128:cw], AF.Exp,
                                                bias=bcol, scale=1.0,
                                                waits=[(DV.ch, bias_ready)])
                                        s_done_ac[p] = t_done[p]
                                        exp_waits = [(AC.ch, t_done[p])]
                                    else:
                                        t_done[p] = AC.op(
                                            "activation",
                                            ebuf[eb][:, el0:el0 + cw],
                                            ps_s[p][:, 0:cw], AF.Exp,
                                            bias=bcol, scale=1.0,
                                            waits=[(PE.ch, mS),
                                                   (DV.ch, bias_ready),
                                                   (PE.ch, e_done[eb])])
                                        s_done_ac[p] = t_done[p]
                                        exp_waits = [(AC.ch, t_done[p])]
                                pend_av.append((kt, imin_h, exp_waits))
                                if len(pend_av) > 2:
                                    pkt, pimin, pw = pend_av.pop(0)
                                    e_done[pkt % 4] = a_done[s] = do_av(
                                        pkt, pw, pimin)
                            while pend_av:
                                pkt, pimin, pw = pend_av.pop(0)
                                e_done[pkt % 4] = a_done[s] = do_av(
                                    pkt, pw, pimin)
                            last_av = PE.ch.n
                            for j in range(4):
                                DV.op("reciprocal", recipb[:, ilo + j:ilo + j + 1],
                                      ps_o4[j][:, 128:129],
                                      waits=[(PE.ch, last_av),
                                             (AC.ch, ps_o_free)])
                            for j in range(4):
                                i = ilo + j
                                DV.op("tensor_scalar",
                                      obuf[:, i * 128:(i + 1) * 128],
                                      ps_o4[j][:, 0:128],
                                      recipb[:, i:i + 1], None, ALU.mult,
                                      waits=[(PE.ch, obuf_free)])
                            AC.op("nop", waits=[(DV.ch, DV.ch.n)])
                            ps_o_free = AC.ch.n
                        bias_free = AC.ch.n

                        def tail(h=h, s=s, pfree=ps_o_free):
                            nonlocal obuf_free
                            for g in range(2):
                                ts = g % 2
                                ins = None
                                for j in range(4):
                                    i = 4 * g + j
                                    ins = PE.raw(
                                        "transpose",
                                        ps_tr0[:, j * 128:(j + 1) * 128],
                                        obuf[:, i * 128:(i + 1) * 128],
                                        identbf[:],
                                        waits=([(AC.ch, pfree),
                                                (DV.ch, tr_done[0])]
                                               if j == 0 else ()))
                                mpe = PE.mark(ins)
                                tr_done[0] = DV.op(
                                    "tensor_copy", otsb[ts][:], ps_tr0[:],
                                    waits=[(PE.ch, mpe),
                                           (gq[2 + ts], ot_done[ts])])
                                PO.dma(gq[2 + ts],
                                       ofm_st[h, :, g * 512:(g + 1) * 512],
                                       otsb[ts][:],
                                       waits=[(DV.ch, tr_done[0]),
                                              (gq[2 + ts], gq[2 + ts].n)])
                                ot_done[ts] = gq[2 + ts].n
                            obuf_free = PE.ch.n
                            q_done[s] = k_done[s] = PE.ch.n
                        pending_tail[0] = tail
                    flush_tail()
                    barrier()

                with contextlib.ExitStack() as e5:
                    ofm = e5.enter_context(
                        nc.sbuf_tensor("ofm", [128, NKC * OWN], BF16))
                    ofv = ofm[:].rearrange("p (k t) -> p k t", k=NKC)
                    for kc in range(NKC):
                        SY.dma(sq[6], ofv[:, kc, :], ofm_st[kc])
                    ofm_m = sq[6].n
                    NSL = 6
                    wpt = [e5.enter_context(
                        nc.sbuf_tensor(f"f5p{s}", [128, 256], F32))
                        for s in range(NSL)]
                    wpb = [e5.enter_context(
                        nc.sbuf_tensor(f"f5pb{s}", [128, 256], BF16))
                        for s in range(NSL)]
                    wft = [e5.enter_context(
                        nc.sbuf_tensor(f"f5f{s}", [128, 256], F32))
                        for s in range(NSL)]
                    wfb = [e5.enter_context(
                        nc.sbuf_tensor(f"f5fb{s}", [128, 256], BF16))
                        for s in range(NSL)]
                    xres = [e5.enter_context(
                        nc.sbuf_tensor(f"f5x{s}", [128, 512], F32))
                        for s in range(2)]
                    t1 = [e5.enter_context(
                        nc.sbuf_tensor(f"f5t1{i}", [128, 512], F32))
                        for i in range(2)]
                    t2 = [e5.enter_context(
                        nc.sbuf_tensor(f"f5t2{i}", [128, 512], F32))
                        for i in range(2)]
                    t3 = e5.enter_context(nc.sbuf_tensor("f5t3", [128, 512], F32))
                    osb = [e5.enter_context(
                        nc.sbuf_tensor(f"f5o{s}", [128, 512], F32))
                        for s in range(2)]
                    ps_a = [e5.enter_context(
                        nc.psum_tensor(f"f5a{i}", [128, 512], F32))
                        for i in range(4)]
                    ps_b = [e5.enter_context(
                        nc.psum_tensor(f"f5b{i}", [128, 512], F32))
                        for i in range(4)]
                    casters = [DV, AC, DV, AC, DV, AC]
                    wp_done = [(DV.ch, 0)] * NSL
                    wf_done = [(DV.ch, 0)] * NSL
                    pb_done = [0] * NSL
                    fb_done = [0] * NSL
                    x_done = [0, 0]
                    o_done = [0, 0]
                    t_free = [0, 0]
                    ev_ac = 0
                    ev_dv = 0
                    for fp_ in range(NKC // 2):
                        for kc in range(NKC):
                            s = (fp_ * NKC + kc) % NSL
                            ce = casters[s]
                            SY.dma(wsq[s], wpt[s][:],
                                   w_proj[kc * 128:(kc + 1) * 128,
                                          fp_ * 256:(fp_ + 1) * 256],
                                   waits=[wp_done[s]])
                            wp_done[s] = (ce.ch, ce.cast(
                                wpb[s][:], wpt[s][:],
                                waits=[(wsq[s], wsq[s].n),
                                       (PE.ch, pb_done[s])]))
                            AC.dma(aq2[s], wft[s][:],
                                   ffn_wT[kc * 128:(kc + 1) * 128,
                                          fp_ * 256:(fp_ + 1) * 256],
                                   waits=[wf_done[s]])
                            wf_done[s] = (ce.ch, ce.cast(
                                wfb[s][:], wft[s][:],
                                waits=[(aq2[s], aq2[s].n),
                                       (PE.ch, fb_done[s])]))
                            ins = None
                            for sub in range(2):
                                for rc in range(2):
                                    ins = PE.raw(
                                        "matmul", ps_a[2 * sub + rc][:],
                                        wpb[s][:, sub * 128:(sub + 1) * 128],
                                        ofv[:, kc, rc * 512:(rc + 1) * 512],
                                        start=(kc == 0),
                                        stop=(kc == NKC - 1),
                                        waits=([wp_done[s],
                                                (AC.ch, ev_ac),
                                                (sq[6], ofm_m)]
                                               if sub == 0 and rc == 0
                                               else ()))
                            pb_done[s] = PE.mark(ins)
                            for sub in range(2):
                                for rc in range(2):
                                    ins = PE.raw(
                                        "matmul", ps_b[2 * sub + rc][:],
                                        wfb[s][:, sub * 128:(sub + 1) * 128],
                                        xnov[:, kc, rc * 512:(rc + 1) * 512],
                                        start=(kc == 0),
                                        stop=(kc == NKC - 1),
                                        waits=([wf_done[s]]
                                               if sub == 0 and rc == 0
                                               else ()))
                            fb_done[s] = PE.mark(ins)
                        mm_done = PE.ch.n
                        for sub in range(2):
                            fc = 2 * fp_ + sub
                            for rc in range(2):
                                so = rc
                                xq = sq[9] if so == 0 else sq[3]
                                SY.dma(xq, xres[so][:],
                                       xT_all[fc * 128:(fc + 1) * 128,
                                              rc * 512:(rc + 1) * 512],
                                       waits=[(DV.ch, x_done[so])])
                                tp = (2 * sub + rc) % 2
                                ev_ac = AC.op(
                                    "activation", t1[tp][:],
                                    ps_a[2 * sub + rc][:], AF.Identity,
                                    bias=bproj[:, fc:fc + 1], scale=1.0,
                                    waits=[(PE.ch, mm_done),
                                           (DV.ch, t_free[tp])])
                                ev_ac = AC.op(
                                    "activation", t2[tp][:],
                                    ps_b[2 * sub + rc][:], AF.Identity,
                                    bias=maskedb[:, fc:fc + 1],
                                    scale=rs_sb[:, fc:fc + 1])
                                ev_dv = DV.op(
                                    "tensor_tensor", t3[:], t2[tp][:],
                                    qselb[:, rc * 512:(rc + 1) * 512],
                                    ALU.mult,
                                    waits=[(AC.ch, ev_ac)])
                                x_done[so] = DV.op(
                                    "tensor_tensor", t1[tp][:], t1[tp][:],
                                    xres[so][:],
                                    ALU.add, waits=[(xq, xq.n)])
                                t_free[tp] = ev_dv = DV.op(
                                    "tensor_tensor", osb[so][:], t1[tp][:],
                                    t3[:],
                                    ALU.add, waits=[(gq[4 + so], o_done[so])])
                                PO.dma(gq[4 + so],
                                       outT[fc * 128:(fc + 1) * 128,
                                            rc * 512:(rc + 1) * 512],
                                       osb[so][:], waits=[(DV.ch, DV.ch.n)])
                                o_done[so] = gq[4 + so].n
                    PO.e.wait_ge(gq[4].h, gq[4].n)
                    PO.e.wait_ge(gq[5].h, gq[5].n)

        block = es.enter_context(nc.Block())

        @block.gpsimd
        def _(_g):
            emit_all()

    return nc


def kernel(L0_attn_hidden_states, L0_attn_w_attn, L0_attn_b_attn,
           L0_attn_w_proj, L0_attn_b_proj, L0_ffn_weight, L0_ffn_bias,
           L0_ffn_centroids, L0_ffn_assignments):
    x = np.ascontiguousarray(np.asarray(L0_attn_hidden_states, np.float32))
    w_attn = np.ascontiguousarray(np.asarray(L0_attn_w_attn, np.float32))
    b_attn = np.asarray(L0_attn_b_attn, np.float32)
    w_proj = np.ascontiguousarray(np.asarray(L0_attn_w_proj, np.float32))
    b_proj = np.asarray(L0_attn_b_proj, np.float32)
    ffn_w = np.asarray(L0_ffn_weight, np.float32)
    ffn_b = np.asarray(L0_ffn_bias, np.float32)
    cent = np.asarray(L0_ffn_centroids, np.float32)
    assign = np.asarray(L0_ffn_assignments, np.int32)

    B = x.shape[0]
    b_attn_t = np.ascontiguousarray(b_attn.reshape(96, 128).T)
    b_proj_t = np.ascontiguousarray(b_proj.reshape(32, 128).T)
    ffn_b_t = np.ascontiguousarray(ffn_b.reshape(32, 128).T)
    ffn_wT = np.ascontiguousarray(ffn_w.T)
    centT = np.ascontiguousarray(cent.T)
    assign_row = np.ascontiguousarray(assign.reshape(1, DM))
    ident = np.eye(128, dtype=np.float32).astype(ml_dtypes.bfloat16)

    bts = []
    for p in range(2):
        qglob = ((2 * (np.arange(OWN) // 128) + p) * 128
                 + (np.arange(OWN) % 128))
        kglob = np.arange(NKT * 128)
        rel = kglob[:, None].astype(np.float64) - qglob[None, :]
        bt = np.where(rel <= 0, rel, NEGBIG).astype(np.float32)
        bts.append(np.ascontiguousarray(bt.reshape(NKT, 128, OWN)))

    bt2s, basecols = [], []
    for p in range(2):
        qcg = [(8 * hf + 3 + p) * 128 + 64 for hf in range(2)]
        bt2 = np.empty((NKT, 128, 128), np.float32)
        base = np.empty((128, 32), np.float32)
        for kt in range(NKT):
            hf = 0 if kt < 8 else 1
            ie = kt // 2
            ki = np.arange(128)
            qe = np.arange(128)
            kglob = kt * 128 + ki
            qglob = (2 * ie + p) * 128 + qe
            val = (kglob[:, None] - qcg[hf]).astype(np.float64)
            mask = kglob[:, None] <= qglob[None, :]
            bt2[kt] = np.where(mask, val, NEGBIG).astype(np.float32)
        for hf in range(2):
            for kt in range(NKT):
                base[:, hf * 16 + kt] = (kt * 128 + np.arange(128)
                                         - qcg[hf]).astype(np.float32)
        bt2s.append(np.ascontiguousarray(bt2))
        basecols.append(np.ascontiguousarray(base))

    x_flat = x.reshape(B * SEQ, DM)
    in_maps = []
    for c in range(8):
        b, p = c // 2, c % 2
        xb = x[b]
        own_rows = xb.reshape(16, 128, DM)[p::2].reshape(OWN, DM)
        own_idx = (np.arange(16)[p::2][:, None] * 128
                   + np.arange(128)[None, :]).reshape(-1) + b * SEQ
        rest_idx = np.setdiff1d(np.arange(B * SEQ), own_idx)
        order = np.concatenate([own_idx, rest_idx])
        xT = np.ascontiguousarray(x_flat[order].T)
        in_maps.append({
            "x_full": xb,
            "x_own": np.ascontiguousarray(own_rows),
            "xT_all": xT,
            "w_attn": w_attn,
            "b_attn_t": b_attn_t,
            "w_proj": w_proj,
            "b_proj_t": b_proj_t,
            "ffn_wT": ffn_wT,
            "ffn_b_t": ffn_b_t,
            "assign_row": assign_row,
            "centT": centT,
            "bt_in": bts[p],
            "bt2e_in": bt2s[p],
            "basecol_in": basecols[p],
            "ident_in": ident,
        })

    nc = build_program()
    res = run_bass_kernel_spmd(nc, in_maps, core_ids=list(range(8)))

    out = np.empty((B, SEQ, DM), np.float32)
    for c in range(8):
        b, p = c // 2, c % 2
        oT = np.asarray(res.results[c]["outT"])
        rows = np.ascontiguousarray(oT.T).reshape(NQT, 128, DM)
        out.reshape(B, 16, 128, DM)[b, p::2] = rows
    return out
